# version 15 (final)
# baseline (speedup 1.0000x reference)
import numpy as np
import ml_dtypes
from contextlib import ExitStack

import concourse.bass as bass
import concourse.mybir as mybir
from concourse.bass_utils import run_bass_kernel_spmd

F32 = mybir.dt.float32
BF16 = mybir.dt.bfloat16
FP8 = mybir.dt.float8e4
I32 = mybir.dt.int32
AX = mybir.AxisListType.X
ALU = mybir.AluOpType
ACT = mybir.ActivationFunctionType
EPS = 1e-5

ENGS = ("sync", "tensor", "vector", "scalar", "gpsimd")


def full_cfg():
    return dict(D=2048, T=2048, FF=8192)


class Sem:
    __slots__ = ("h", "c")

    def __init__(self, stack, nc, name):
        self.h = stack.enter_context(nc.semaphore(name))
        self.c = 0

    def bump(self, n):
        self.c += n
        return self.c


def build_program(cfg):
    D, T, FF = cfg["D"], cfg["T"], cfg["FF"]
    NT = T // 128
    NE = D // 128
    NDT = D // 128
    NFFD = FF // 128
    NFC = FF // 512
    NCH = T // 512
    QW = 1024 if (D % 1024 == 0 and FF % 1024 == 0) else 512
    NWINV = 1.0 / (D * FF)
    SEQ = 2 * T
    assert NT % 2 == 0 and NCH >= 1 and D % QW == 0 and FF % QW == 0

    nc = bass.Bass("TRN2", target_bir_lowering=False)

    xm_d = nc.dram_tensor("xm", [T, D], F32, kind="ExternalInput")
    xp_d = nc.dram_tensor("xp", [T, D], F32, kind="ExternalInput")
    wq_d = nc.dram_tensor("wq", [NE, 128, D], F32, kind="ExternalInput")
    wk_d = nc.dram_tensor("wk", [NE, 128, D], F32, kind="ExternalInput")
    wv_d = nc.dram_tensor("wv", [NE, 128, D], F32, kind="ExternalInput")
    wo_d = nc.dram_tensor("wo", [D // 512, 128, NE * 512], F32, kind="ExternalInput")
    wg_d = nc.dram_tensor("wg", [NDT, 128, FF], F32, kind="ExternalInput")
    wu_d = nc.dram_tensor("wu", [NDT, 128, FF], F32, kind="ExternalInput")
    wd_d = nc.dram_tensor("wd", [NFFD, 128, D], F32, kind="ExternalInput")
    id32_d = nc.dram_tensor("id32", [128, 128], F32, kind="ExternalInput")
    ones_d = nc.dram_tensor("ones16", [128, 2], BF16, kind="ExternalInput")
    onesr_d = nc.dram_tensor("onesr16", [1, 128], BF16, kind="ExternalInput")
    consts_d = nc.dram_tensor("consts", [128, 8], F32, kind="ExternalInput")
    out_d = nc.dram_tensor("out", [T, D], F32, kind="ExternalOutput")

    vhat_d = nc.dram_tensor("vhat_s", [2, NE, 128, T], BF16)
    h_d = nc.dram_tensor("h_s", [NFFD, 128, T], F32)
    wg8_d = nc.dram_tensor("wg8_s", [NFC, NDT, 128, 512], BF16)
    wu8_d = nc.dram_tensor("wu8_s", [NFC, NDT, 128, 512], BF16)
    wd8_d = nc.dram_tensor("wd8_s", [NFFD, 128, D], BF16)

    with ExitStack() as stack:
        sb = lambda nm, shp, dt=F32: stack.enter_context(nc.sbuf_tensor(nm, shp, dt))

        S1W = max(NDT * T, NE * D, NFFD * 256 + 4 * D)
        S2W = max(NDT * T, NE * D, NFFD * 256 + 2 * 4 * D, 4 * NDT * 512)
        A32 = sb("A32", [128, 10240])
        S1 = sb("S1", [128, S1W], BF16)
        S2 = sb("S2", [128, S2W], BF16)
        A16 = sb("A16", [128, 16384], BF16)
        AI = sb("AI", [128, max(QW, 1024)], I32)
        id32 = sb("id32s", [128, 128])
        ones = sb("ones16s", [128, 2], BF16)
        onesr = sb("onesr16s", [1, 128], BF16)
        consts = sb("constss", [128, 8])
        nc.const_aps.register(F32, EPS, consts[:, 0:1])
        nc.const_aps.register(F32, 1.0 / D, consts[:, 1:2])
        nc.const_aps.register(F32, 1.0, consts[:, 4:5])

        tC = sb("tC", [128, NE])
        tM = sb("tM", [128, NT])
        tNM = sb("tNM", [128, NT])
        tR = sb("tR", [128, NT])
        tRM = sb("tRM", [128, NT])
        tRR = sb("tRR", [128, NT])
        tSCG = sb("tSCG", [128, NT])
        tSCU = sb("tSCU", [128, NT])
        tSCD = sb("tSCD", [128, NT])
        tHM = sb("tHM", [128, NT])
        tS = sb("tS", [128, 12])
        tSn = sb("tSn", [128, 4])
        tACC = sb("tACC", [128, 4])
        tT1 = sb("tT1", [128, 8])
        tT2 = sb("tT2", [128, 8])
        tSum = sb("tSum", [1, 4])
        tSplit = sb("tSplit", [128, 4], BF16)

        pb = [stack.enter_context(nc.psum_tensor(f"pb{i}", [128, 512], F32))
              for i in range(8)]

        ALL_SEMS = []

        def mksem(name):
            s = Sem(stack, nc, name)
            ALL_SEMS.append(s)
            return s

        smm = mksem("smm")
        ssc = mksem("ssc")
        svv = mksem("svv")
        sgc = mksem("sgc")
        CSEM = {"smm": smm, "ssc": ssc, "svv": svv, "sgc": sgc}

        S = {k: [] for k in ENGS}

        def em(eng, fn):
            S[eng].append(fn)

        def wait(eng, sem, val):
            if val and val > 0:
                em(eng, lambda e, s=sem.h, v=val: e.wait_ge(s, v))

        def wait_t(eng, tok):
            if tok:
                wait(eng, tok[0], tok[1])

        SELF = {"vector": svv, "scalar": ssc, "gpsimd": sgc}

        def op(eng, fn, sem=None, n=1):
            if eng in SELF:
                s = SELF[eng]
                assert sem is None or sem is s
                wait(eng, s, s.c)
                t = s.bump(1)
                em(eng, lambda e, f=fn, sh=s.h: f(e).then_inc(sh, 1))
                return t
            if sem is None:
                em(eng, fn)
                return None
            t = sem.bump(n)
            em(eng, lambda e, f=fn, s=sem.h, k=n: f(e).then_inc(s, k))
            return t

        def topi(fn):
            t = smm.bump(1)
            em("tensor", lambda e, f=fn, sh=smm.h: f(e).then_inc(sh, 1))
            return t

        class DS:

            def __init__(self, name, eng, nsem=2):
                self.sems = [mksem(f"{name}{i}") for i in range(nsem)]
                self.eng = eng
                self.i = 0

            def dma(self, out, in_, **kw):
                s = self.sems[self.i % len(self.sems)]
                self.i += 1
                wait(self.eng, s, s.c)
                t = s.bump(16)
                em(self.eng, lambda e, o=out, i_=in_, kw=dict(kw), sh=s.h:
                   e.dma_start(out=o, in_=i_, **kw).then_inc(sh, 16))
                return (s, t)

            def all_toks(self):
                return [(s, s.c) for s in self.sems]

        def wait_all(eng, stream):
            for tok in stream.all_toks():
                wait_t(eng, tok)

        stX = DS("sx", "sync", 3)
        stWA = DS("swa", "sync", 3)
        stWM = DS("swm", "sync", 3)
        stWQ = DS("swq", "sync", 3)
        stWO = DS("swo", "sync", 2)
        stVR = DS("svr", "sync", 4)
        stHR = DS("shr", "scalar", 3)
        stVW = DS("svw", "scalar", 2)
        stHW = DS("shw", "scalar", 2)
        stOW = DS("sow", "scalar", 2)
        stQ8W = DS("sq8w", "scalar", 2)
        stQ8R = DS("sq8r", "sync", 3)
        stOA = DS("soa", "gpsimd", 2)

        def phase_barrier():
            vals = [(s, s.c) for s in ALL_SEMS]
            for eng in ENGS:
                for s_, v_ in vals:
                    wait(eng, s_, v_)

        for o, i in ((id32[:, :], id32_d[:, :]), (ones[:, :], ones_d[:, :]),
                     (onesr[:, :], onesr_d[:, :]), (consts[:, :], consts_d[:, :])):
            stX.dma(o, i)
        phase_barrier()

        wimgs = [(wg_d, NDT, FF, 0), (wu_d, NDT, FF, 1), (wd_d, NFFD, D, 2)]
        mean_units = []
        for wdd, nblk, blkw, wi in wimgs:
            for bi in range(nblk):
                for c in range(blkw // QW):
                    mean_units.append((wdd[bi, :, c * QW:(c + 1) * QW], wi))
        wm_slot = [A32[:, 6144:6144 + QW], A32[:, 7168:7168 + QW],
                   A32[:, 8192:8192 + QW], A32[:, 9216:9216 + QW]]
        mu = {"i": 0, "load_t": {}, "free_t": {}}

        def mean_issue_load(idx):
            if idx >= len(mean_units):
                return
            src, _ = mean_units[idx]
            slot = idx % 4
            ft = mu["free_t"].get(slot)
            if ft:
                wait("sync", svv, ft)
            mu["load_t"][idx] = stWM.dma(wm_slot[slot], src)

        def mean_step(nunits):
            for _ in range(nunits):
                i = mu["i"]
                if i >= len(mean_units):
                    return
                if i == 0:
                    for k_ in range(4):
                        mean_issue_load(k_)
                _, wi = mean_units[i]
                slot = i % 4
                wait_t("vector", mu["load_t"][i])
                op("vector", lambda e, s=wm_slot[slot]:
                   e.tensor_reduce(out=tT1[:, 0:1], in_=s, axis=AX, op=ALU.add,
                                   apply_absolute_value=True))
                ft = op("vector", lambda e, wi=wi:
                        e.tensor_tensor(out=tACC[:, wi:wi + 1],
                                        in0=tACC[:, wi:wi + 1],
                                        in1=tT1[:, 0:1], op=ALU.add), svv)
                mu["free_t"][slot] = ft
                mean_issue_load(i + 4)
                mu["i"] += 1

        op("vector", lambda e: e.memset(tACC[:, 0:4], 0.0), svv)

        q_units = []
        for wdd, nblk, blkw, wi in wimgs:
            for bi in range(nblk):
                for c in range(blkw // QW):
                    src = wdd[bi, :, c * QW:(c + 1) * QW]
                    if wi < 2:
                        w8 = wg8_d if wi == 0 else wu8_d
                        dsts = [w8[(c * QW) // 512 + j, bi, :, :]
                                for j in range(QW // 512)]
                    else:
                        dsts = [wd8_d[bi, :, c * QW:(c + 1) * QW]]
                    q_units.append((src, wi, dsts))
        q_in_sl = [A32[:, 8192:8192 + QW], A32[:, 9216:9216 + QW]]
        qp_buf = A16[:, 14336:14336 + QW]
        qn_buf = A16[:, 15360:15360 + QW]
        qb_buf = A16[:, 12288:12288 + QW]
        qu = {"i": 0, "wq_free": None, "gu_done": None, "th_ready": 0}

        def quant_step(nunits):
            for _ in range(nunits):
                i = qu["i"]
                if i >= len(q_units):
                    return
                src, wi, dsts = q_units[i]
                q_in = q_in_sl[i % 2]
                wait("sync", svv, qu.get("b_done_%d" % (i % 2), 0))
                lt = stWQ.dma(q_in, src)
                wait_t("vector", lt)
                op("vector", lambda e, wi=wi, q_in=q_in:
                   e.tensor_scalar(out=qp_buf, in0=q_in,
                                   scalar1=tS[:, 3 + wi:4 + wi], scalar2=None,
                                   op0=ALU.is_gt))
                nt_ = op("vector", lambda e, wi=wi, q_in=q_in:
                         e.tensor_scalar(out=qn_buf, in0=q_in,
                                         scalar1=tSn[:, wi:wi + 1], scalar2=None,
                                         op0=ALU.is_lt))
                qu["b_done_%d" % (i % 2)] = nt_
                for tok_ in (qu["wq_free"] or []):
                    wait_t("vector", tok_)
                bt = op("vector", lambda e:
                        e.tensor_tensor(out=qb_buf, in0=qp_buf, in1=qn_buf,
                                        op=ALU.subtract), svv)

                wait("scalar", svv, bt)
                nj = len(dsts)
                w = QW // nj
                wts = []
                for j, dst in enumerate(dsts):
                    wts.append(stQ8W.dma(dst, qb_buf[:, j * w:(j + 1) * w]))
                qu["wq_free"] = wts
                if wi < 2:
                    qu["gu_done"] = wts[-1]
                qu["i"] += 1

        xslot = [A32[:, 0:2048], A32[:, 2048:4096]]
        xnslot = [A32[:, 4096:6144], A32[:, 6144:8192]]
        ln = {"xfree": [[0, 0], [0, 0]], "pair_tr": 0, "bankf": [0, 0]}

        def ln_phase(xd, save_stats):
            for pair in range(NT // 2):
                vts = {}
                sts = {}
                xts = {}
                for half in range(2):
                    i = pair * 2 + half
                    sl = half
                    wait("sync", svv, ln["xfree"][sl][0])
                    wait("sync", ssc, ln["xfree"][sl][1])
                    xt = stX.dma(xslot[sl][:, 0:D], xd[i * 128:(i + 1) * 128, 0:D])
                    xts[i] = xt
                    wait_t("vector", xt)
                    op("vector", lambda e, sl=sl:
                       e.reduce_sum(out=tT1[:, 0:1], in_=xslot[sl][:, 0:D], axis=AX))
                    op("vector", lambda e, i=i:
                       e.tensor_scalar(out=tM[:, i:i + 1], in0=tT1[:, 0:1],
                                       scalar1=1.0 / D, scalar2=None, op0=ALU.mult))
                    vt = op("vector", lambda e, i=i:
                            e.tensor_scalar(out=tNM[:, i:i + 1], in0=tT1[:, 0:1],
                                            scalar1=-1.0 / D, scalar2=None,
                                            op0=ALU.mult), svv)
                    vts[i] = vt
                    wait("scalar", svv, vt)
                    wait("scalar", smm, ln["pair_tr"])
                    wait_t("scalar", xt)
                    op("scalar", lambda e, sl=sl, i=i:
                       e.activation(out=xnslot[sl][:, 0:D], in_=xslot[sl][:, 0:D],
                                    func=ACT.Square, bias=tNM[:, i:i + 1],
                                    accum_out=tT1[:, 1 + sl:2 + sl]))
                    st_ = op("scalar", lambda e, sl=sl:
                             e.activation(out=tT2[:, 4 + sl:5 + sl],
                                          in_=tT1[:, 1 + sl:2 + sl],
                                          func=ACT.Sqrt, scale=1.0 / D, bias=EPS),
                             ssc)
                    sts[i] = st_
                for half in range(2):
                    i = pair * 2 + half
                    sl = half
                    wait("vector", ssc, sts[i])
                    wait("vector", smm, ln["pair_tr"])
                    op("vector", lambda e, i=i, sl=sl:
                       e.reciprocal(out=tR[:, i:i + 1], in_=tT2[:, 4 + sl:5 + sl]))
                    xnt = op("vector", lambda e, sl=sl, i=i:
                             e.tensor_scalar(out=xnslot[sl][:, 0:D],
                                             in0=xslot[sl][:, 0:D],
                                             scalar1=tM[:, i:i + 1],
                                             scalar2=tR[:, i:i + 1],
                                             op0=ALU.subtract, op1=ALU.mult), svv)
                    if save_stats:
                        op("vector", lambda e, sl=sl, i=i:
                           e.tensor_reduce(out=tRM[:, i:i + 1],
                                           in_=xnslot[sl][:, 0:D], axis=AX,
                                           op=ALU.max, apply_absolute_value=True))
                    ln["xfree"][sl] = [xnt, sts[i]]
                xn_ready = svv.c
                wait("tensor", svv, xn_ready)
                for f in range(NE):
                    bk = 4 + (f % 2)
                    wait("tensor", svv, ln["bankf"][f % 2])
                    op("tensor", lambda e, bk=bk, f=f:
                       e.transpose(pb[bk][:, 0:128],
                                   xnslot[0][:, f * 128:(f + 1) * 128],
                                   id32[:, :]))
                    mt = topi(lambda e, bk=bk, f=f:
                              e.transpose(pb[bk][:, 128:256],
                                          xnslot[1][:, f * 128:(f + 1) * 128],
                                          id32[:, :]))
                    wait("scalar", smm, mt)
                    base = f * T + pair * 256
                    ht = op("scalar", lambda e, bk=bk, base=base:
                            e.copy(out=S1[:, base:base + 256],
                                   in_=pb[bk][:, 0:256]), ssc)
                    wait("vector", ssc, ht)
                    lt_ = op("vector", lambda e, bk=bk, base=base:
                             e.scalar_tensor_tensor(out=S2[:, base:base + 256],
                                                    in0=pb[bk][:, 0:256], scalar=0.0,
                                                    in1=S1[:, base:base + 256],
                                                    op0=ALU.add, op1=ALU.subtract),
                             svv)
                    ln["bankf"][f % 2] = lt_
                ln["pair_tr"] = smm.c

        ln_phase(xp_d, False)

        phase_barrier()
        w_in = A32[:, 0:2048]
        kbuf = A32[:, 2048:4096]
        kvbuf = A32[:, 4096:6144]
        whi = [A16[:, 0:2048], A16[:, 2048:4096]]
        wlo = [A16[:, 4096:6144], A16[:, 6144:8192]]
        vh_hi = A16[:, 8192:8192 + 2048]
        vh_lo = A16[:, 10240:10240 + 2048]

        sp = {"w_free": 0, "whilo_free": [0, 0]}

        def stream_weight_split(wdram, e, par):
            wait("sync", svv, sp["w_free"])
            lt = stWA.dma(w_in[:, 0:D], wdram[e, :, :])
            wait("vector", smm, sp["whilo_free"][par])
            wait_t("vector", lt)
            op("vector", lambda e_, p=par:
               e_.tensor_copy(whi[p][:, 0:D], w_in[:, 0:D]))
            st = op("vector", lambda e_, p=par:
                    e_.scalar_tensor_tensor(out=wlo[p][:, 0:D], in0=w_in[:, 0:D],
                                            scalar=0.0, in1=whi[p][:, 0:D],
                                            op0=ALU.add, op1=ALU.subtract), svv)
            sp["w_free"] = st
            return st

        def proj_mms(par, bank_free, split_t):
            res = []
            wait("tensor", svv, split_t)
            for c in range(NCH):
                bf_val, bf_sem = bank_free[c]
                if bf_val:
                    wait("tensor", CSEM[bf_sem], bf_val)
                for dt in range(NDT):
                    mv = dt * T + c * 512
                    op("tensor", lambda e, c=c, dt=dt, p=par, mv=mv, s=(dt == 0):
                       e.matmul(pb[c][:, :], whi[p][:, dt * 128:(dt + 1) * 128],
                                S1[:, mv:mv + 512], start=s, stop=False))
                    op("tensor", lambda e, c=c, dt=dt, p=par, mv=mv:
                       e.matmul(pb[c][:, :], whi[p][:, dt * 128:(dt + 1) * 128],
                                S2[:, mv:mv + 512], start=False, stop=False))
                    lam = (lambda e, c=c, dt=dt, p=par, mv=mv,
                           s=(dt == NDT - 1):
                           e.matmul(pb[c][:, :],
                                    wlo[p][:, dt * 128:(dt + 1) * 128],
                                    S1[:, mv:mv + 512], start=False, stop=s))
                    if dt == NDT - 1:
                        last_t = topi(lam)
                    else:
                        op("tensor", lam)
                res.append((c, last_t))
            sp["whilo_free"][par] = smm.c
            return res

        bank_free = [(0, "svv")] * NCH
        next_k = next_v = None
        for e in range(NE):
            st_k = next_k if next_k is not None else \
                stream_weight_split(wk_d, e, 0)
            st_v = next_v if next_v is not None else \
                stream_weight_split(wv_d, e, 1)
            kres = proj_mms(0, bank_free, st_k)
            kev = []
            for c, mt in kres:
                wait("scalar", smm, mt)
                kev.append(op("scalar", lambda e_, c=c:
                              e_.copy(out=kbuf[:, c * 512:(c + 1) * 512],
                                      in_=pb[c][:, :]), ssc))
            vres = proj_mms(1, [(kev[c], "ssc") for c in range(NCH)], st_v)
            kvts = []
            for c, mt in vres:
                wait("vector", smm, mt)
                wait("vector", ssc, kev[c])
                kvts.append(op("vector", lambda e_, c=c:
                              e_.tensor_tensor(out=kvbuf[:, c * 512:(c + 1) * 512],
                                               in0=kbuf[:, c * 512:(c + 1) * 512],
                                               in1=pb[c][:, :], op=ALU.mult), svv))
            bank_free = [(kvts[c], "svv") for c in range(NCH)]
            op("vector", lambda e_, e=e:
               e_.tensor_reduce(out=tC[:, e:e + 1], in_=kvbuf[:, 0:T], axis=AX,
                                op=ALU.add), svv)
            if e + 1 < NE:
                next_k = stream_weight_split(wk_d, e + 1, 0)
                next_v = stream_weight_split(wv_d, e + 1, 1)
            mean_step(len(mean_units) // NE + 1)
        mean_step(len(mean_units))

        ph1_mm = smm.c
        phase_barrier()
        ln["pair_tr"] = ph1_mm
        ln["xfree"] = [[svv.c, ssc.c], [svv.c, ssc.c]]
        ln_phase(xm_d, True)

        op("vector", lambda e: e.tensor_scalar(out=tRM[:, 0:NT], in0=tRM[:, 0:NT],
                                               scalar1=EPS, scalar2=None,
                                               op0=ALU.max))
        op("vector", lambda e: e.reciprocal(out=tT2[:, 0:1], in_=consts[:, 2:3]))
        op("vector", lambda e: e.reciprocal(out=tRR[:, 0:NT], in_=tRM[:, 0:NT]))
        op("vector", lambda e: e.tensor_scalar(out=tRR[:, 0:NT], in0=tRR[:, 0:NT],
                                               scalar1=consts[:, 2:3], scalar2=None,
                                               op0=ALU.mult))
        op("vector", lambda e: e.tensor_tensor(out=tRR[:, 0:NT], in0=tRR[:, 0:NT],
                                               in1=tR[:, 0:NT], op=ALU.mult))
        sden = op("scalar", lambda e: e.activation(out=tT2[:, 1:2],
                                                   in_=consts[:, 3:4],
                                                   func=ACT.Sqrt))
        wait("vector", ssc, sden)
        op("vector", lambda e: e.tensor_scalar(out=tS[:, 6:7], in0=tT2[:, 1:2],
                                               scalar1=1.0 / SEQ, scalar2=None,
                                               op0=ALU.mult), svv)

        def split3(src, prow=slice(0, 128)):
            op("vector", lambda e: e.tensor_copy(tSplit[prow, 0:1], src))
            op("vector", lambda e: e.scalar_tensor_tensor(
                out=tT2[prow, 2:3], in0=src, scalar=0.0, in1=tSplit[prow, 0:1],
                op0=ALU.add, op1=ALU.subtract))
            op("vector", lambda e: e.tensor_copy(tSplit[prow, 1:2], tT2[prow, 2:3]))
            op("vector", lambda e: e.scalar_tensor_tensor(
                out=tT2[prow, 3:4], in0=tT2[prow, 2:3], scalar=0.0,
                in1=tSplit[prow, 1:2], op0=ALU.add, op1=ALU.subtract))
            return op("vector", lambda e: e.tensor_copy(tSplit[prow, 2:3],
                                                        tT2[prow, 3:4]), svv)

        for wi in range(3):
            st3 = split3(tACC[:, wi:wi + 1])
            wait("tensor", svv, st3)
            for j in range(2):
                op("tensor", lambda e, j=j: e.matmul(
                    pb[4][0:1, 0:1], ones[:, 0:1], tSplit[:, j:j + 1],
                    start=(j == 0), stop=False))
            mt = topi(lambda e: e.matmul(
                pb[4][0:1, 0:1], ones[:, 0:1], tSplit[:, 2:3],
                start=False, stop=True))
            wait("scalar", smm, mt)
            sc1 = op("scalar", lambda e, wi=wi:
                     e.copy(out=tSum[0:1, wi:wi + 1], in_=pb[4][0:1, 0:1]), ssc)
            wait("vector", ssc, sc1)
            op("vector", lambda e, wi=wi: e.tensor_scalar(
                out=tSum[0:1, wi:wi + 1], in0=tSum[0:1, wi:wi + 1],
                scalar1=NWINV, scalar2=EPS, op0=ALU.mult, op1=ALU.add))
            st3b = split3(tSum[0:1, wi:wi + 1], prow=slice(0, 1))
            wait("tensor", svv, st3b)
            for j in range(2):
                op("tensor", lambda e, j=j: e.matmul(
                    pb[5][:, 0:1], onesr[:, :], tSplit[0:1, j:j + 1],
                    start=(j == 0), stop=False))
            mt2 = topi(lambda e: e.matmul(
                pb[5][:, 0:1], onesr[:, :], tSplit[0:1, 2:3],
                start=False, stop=True))
            wait("scalar", smm, mt2)
            sc2 = op("scalar", lambda e, wi=wi:
                     e.copy(out=tS[:, wi:wi + 1], in_=pb[5][:, 0:1]), ssc)
            wait("vector", ssc, sc2)
            op("vector", lambda e, wi=wi: e.tensor_scalar(
                out=tS[:, 3 + wi:4 + wi], in0=tS[:, wi:wi + 1], scalar1=0.5,
                scalar2=None, op0=ALU.mult))
            op("vector", lambda e, wi=wi: e.tensor_scalar(
                out=tSn[:, wi:wi + 1], in0=tS[:, wi:wi + 1], scalar1=-0.5,
                scalar2=None, op0=ALU.mult))
            op("vector", lambda e, wi=wi: e.tensor_tensor(
                out=tS[:, 8 + wi:9 + wi], in0=tS[:, wi:wi + 1], in1=tT2[:, 0:1],
                op=ALU.mult), svv)
        qu["th_ready"] = svv.c
        op("vector", lambda e: e.tensor_scalar(out=tSCG[:, 0:NT], in0=tRM[:, 0:NT],
                                               scalar1=tS[:, 8:9], scalar2=None,
                                               op0=ALU.mult))
        op("vector", lambda e: e.tensor_scalar(out=tSCU[:, 0:NT], in0=tRM[:, 0:NT],
                                               scalar1=tS[:, 9:10], scalar2=None,
                                               op0=ALU.mult))
        op("vector", lambda e: e.memset(tHM[:, 0:NT], 0.0), svv)

        phase_barrier()
        qbuf = A32[:, 2048:4096]
        kbuf3 = A32[:, 4096:6144]
        kvbuf3 = A32[:, 6144:8192]

        vhat_free = None
        q_consumed = 0
        k_consumed = 0
        bank_free = [(svv.c, "svv")] * NCH
        next_split = None
        next_ksplit = None
        for e in range(NE):
            st_q = next_split if next_split is not None else \
                stream_weight_split(wq_d, e, 0)
            qres = proj_mms(0, bank_free, st_q)
            qev = []
            for c, mt in qres:
                wait("scalar", smm, mt)
                if c == 0:
                    wait("scalar", svv, q_consumed)
                qev.append(op("scalar", lambda e_, c=c:
                              e_.copy(out=qbuf[:, c * 512:(c + 1) * 512],
                                      in_=pb[c][:, :]), ssc))
            st_k = next_ksplit if next_ksplit is not None else \
                stream_weight_split(wk_d, e, 1)
            kres = proj_mms(1, [(qev[c], "ssc") for c in range(NCH)], st_k)
            kev = []
            for c, mt in kres:
                wait("scalar", smm, mt)
                if c == 0:
                    wait("scalar", svv, k_consumed)
                kev.append(op("scalar", lambda e_, c=c:
                              e_.copy(out=kbuf3[:, c * 512:(c + 1) * 512],
                                      in_=pb[c][:, :]), ssc))
            st_v = stream_weight_split(wv_d, e, 0)
            vres = proj_mms(0, [(kev[c], "ssc") for c in range(NCH)], st_v)
            if e + 1 < NE:
                next_split = stream_weight_split(wq_d, e + 1, 0)
                next_ksplit = stream_weight_split(wk_d, e + 1, 1)
            kvts = []
            for c, mt in vres:
                wait("vector", smm, mt)
                wait("vector", ssc, kev[c])
                if c == 0 and vhat_free is not None:
                    wait_t("vector", vhat_free[0])
                    wait_t("vector", vhat_free[1])
                kvts.append(op("vector", lambda e_, c=c:
                              e_.tensor_tensor(out=kvbuf3[:, c * 512:(c + 1) * 512],
                                               in0=kbuf3[:, c * 512:(c + 1) * 512],
                                               in1=pb[c][:, :], op=ALU.mult), svv))
            bank_free = [(kvts[c], "svv") for c in range(NCH)]
            k_consumed = kvts[-1]
            op("vector", lambda e_, e=e: e_.tensor_tensor_scan(
                out=kvbuf3[:, 0:T], data0=kvbuf3[:, 0:T], data1=kvbuf3[:, 0:T],
                initial=tC[:, e:e + 1], op0=ALU.add, op1=ALU.bypass))
            q_consumed = op("vector", lambda e_: e_.scalar_tensor_tensor(
                out=kvbuf3[:, 0:T], in0=qbuf[:, 0:T], scalar=tS[:, 6:7],
                in1=kvbuf3[:, 0:T], op0=ALU.mult, op1=ALU.mult), svv)
            op("vector", lambda e_: e_.tensor_copy(vh_hi[:, 0:T], kvbuf3[:, 0:T]))
            spl = op("vector", lambda e_: e_.scalar_tensor_tensor(
                out=vh_lo[:, 0:T], in0=kvbuf3[:, 0:T], scalar=0.0,
                in1=vh_hi[:, 0:T], op0=ALU.add, op1=ALU.subtract), svv)
            wait("scalar", svv, spl)
            vw1 = stVW.dma(vhat_d[0, e, :, :], vh_hi[:, 0:T])
            vw2 = stVW.dma(vhat_d[1, e, :, :], vh_lo[:, 0:T])
            vhat_free = (vw1, vw2)
            quant_step((2 * len(q_units) // 3 + NE - 1) // NE)

        ph3_mm = smm.c

        phase_barrier()
        wo_in = A32[:, 0:2048]
        stage4 = A32[:, 2048:4096]
        x5sl = [A32[:, 4096:6144]]
        t5 = A32[:, 6144:6144 + QW]
        c5 = A32[:, 7168:7168 + QW]
        d5 = A32[:, 8192:8192 + QW]
        r5 = A32[:, 9216:9216 + QW]
        NSL = D // QW

        vs_hi_sl = [A16[:, 0:NE * 256], A16[:, 2 * NE * 256:3 * NE * 256]]
        vs_lo_sl = [A16[:, NE * 256:2 * NE * 256],
                    A16[:, 3 * NE * 256:4 * NE * 256]]
        NTTG = NT // 2
        NFCW = D // 512

        prep_units = [(tt, s) for tt in range(NT) for s in range(NSL)]
        pstate = {"i": 0, "x_free": 0, "x_load": None, "af_free": 0,
                  "tb_free": [0, 0]}

        def prep_step(n):
            for _ in range(n):
                i = pstate["i"]
                if i >= len(prep_units):
                    return
                tt, s = prep_units[i]
                x5 = x5sl[0]
                if s == 0:
                    wait("sync", svv, pstate["x_free"])
                    pstate["x_load"] = stX.dma(
                        x5[:, 0:D], xm_d[tt * 128:(tt + 1) * 128, 0:D])
                wait_t("vector", pstate["x_load"])
                op("vector", lambda e, s=s, tt=tt, x5=x5: e.tensor_scalar(
                    out=t5, in0=x5[:, s * QW:(s + 1) * QW],
                    scalar1=tM[:, tt:tt + 1], scalar2=tRR[:, tt:tt + 1],
                    op0=ALU.subtract, op1=ALU.mult))
                op("vector", lambda e: e.tensor_copy(AI[:, 0:QW], t5))
                op("vector", lambda e: e.tensor_copy(r5, AI[:, 0:QW]))
                wait("vector", ssc, pstate["af_free"])
                op("vector", lambda e: e.scalar_tensor_tensor(
                    out=c5, in0=t5, scalar=-0.5, in1=r5,
                    op0=ALU.add, op1=ALU.is_gt))
                op("vector", lambda e: e.scalar_tensor_tensor(
                    out=d5, in0=t5, scalar=0.5, in1=r5,
                    op0=ALU.add, op1=ALU.is_lt))
                op("vector", lambda e: e.tensor_tensor(out=t5, in0=r5,
                                                       in1=c5, op=ALU.add))
                op("vector", lambda e: e.tensor_tensor(out=r5, in0=t5, in1=d5,
                                                       op=ALU.subtract))
                cl = op("vector", lambda e: e.tensor_scalar(
                    out=c5, in0=r5, scalar1=127.0, scalar2=-128.0,
                    op0=ALU.min, op1=ALU.max), svv)
                if s == NSL - 1:
                    pstate["x_free"] = cl
                ngr = QW // 512
                for g in range(ngr):
                    tb = (g + s * ngr) % 2
                    wait("tensor", ssc, pstate["tb_free"][tb])
                    wait("tensor", svv, cl)
                    for j in range(4):
                        col = g * 512 + j * 128
                        lam = (lambda e, col=col, tb=tb, j=j:
                               e.transpose(pb[4 + tb][:, j * 128:(j + 1) * 128],
                                           c5[:, col:col + 128], id32[:, :]))
                        if j == 3:
                            mt2 = topi(lam)
                        else:
                            op("tensor", lam)
                    wait("scalar", smm, mt2)
                    dt0 = s * (QW // 128) + g * 4
                    dst = bass.AP(S1, dt0 * T + tt * 128,
                                  [[S1W, 128], [T, 4], [1, 128]])
                    ev = op("scalar", lambda e, tb=tb, dst=dst:
                            e.copy(out=dst, in_=pb[4 + tb][:, 0:512]), ssc)
                    pstate["tb_free"][tb] = ev
                    pstate["af_free"] = ev
                pstate["i"] += 1

        fc_pairs = [tuple(range(p, min(p + 2, NFCW)))
                    for p in range(0, NFCW, 2)]
        bank_free4 = [(0, "ssc")] * 8
        stage_free = [None] * 4
        wo_sweep_done = 0
        n_prep_per_iter = max(1, (len(prep_units) + len(fc_pairs) * NTTG - 1)
                              // (len(fc_pairs) * NTTG))
        for pi, fcl in enumerate(fc_pairs):
            spl_done = 0
            for c2 in range(len(fcl)):
                fc = fcl[c2]
                for hhalf in range(NE * 512 // 2048):
                    wait("sync", svv, spl_done)
                    wait("sync", smm, wo_sweep_done)
                    lt = stWO.dma(wo_in[:, 0:2048],
                                  wo_d[fc, :, hhalf * 2048:(hhalf + 1) * 2048])
                    wait_t("vector", lt)
                    wait("vector", smm, wo_sweep_done)
                    base = c2 * 2 * NE * 512 + hhalf * 2048
                    op("vector", lambda e, base=base:
                       e.tensor_copy(S2[:, base:base + 2048], wo_in[:, 0:2048]))
                    spl_done = op("vector", lambda e, base=base:
                                  e.scalar_tensor_tensor(
                                      out=S2[:, NE * 512 + base:
                                             NE * 512 + base + 2048],
                                      in0=wo_in[:, 0:2048], scalar=0.0,
                                      in1=S2[:, base:base + 2048],
                                      op0=ALU.add, op1=ALU.subtract), svv)
            pair_split_done = spl_done
            for ttg in range(NTTG):
                gi = pi * NTTG + ttg
                par2 = gi % 2
                vs_hi = vs_hi_sl[par2]
                vs_lo = vs_lo_sl[par2]
                if gi == 0:
                    wait("sync", smm, pstate.get("vs_free_%d" % par2, 0))
                    for et in range(NE):
                        stVR.dma(vs_hi[:, et * 256:(et + 1) * 256],
                                 vhat_d[0, et, :, ttg * 256:(ttg + 1) * 256])
                        vs_tok = stVR.dma(
                            vs_lo[:, et * 256:(et + 1) * 256],
                            vhat_d[1, et, :, ttg * 256:(ttg + 1) * 256])
                    pstate["vs_toks_%d" % par2] = stVR.all_toks()
                ngi = gi + 1
                if ngi < len(fc_pairs) * NTTG:
                    npar = ngi % 2
                    nttg = ngi % NTTG
                    wait("sync", smm, pstate.get("vs_free_%d" % npar, 0))
                    for et in range(NE):
                        stVR.dma(vs_hi_sl[npar][:, et * 256:(et + 1) * 256],
                                 vhat_d[0, et, :, nttg * 256:(nttg + 1) * 256])
                        stVR.dma(vs_lo_sl[npar][:, et * 256:(et + 1) * 256],
                                 vhat_d[1, et, :, nttg * 256:(nttg + 1) * 256])
                    pstate["vs_toks_%d" % npar] = stVR.all_toks()
                for tok_ in pstate["vs_toks_%d" % par2]:
                    wait_t("tensor", tok_)
                wait("tensor", svv, pair_split_done)
                for tl_ in range(2):
                    for c2_ in range(len(fcl)):
                        bf_val, bf_sem = bank_free4[tl_ * 2 + c2_]
                        if bf_val:
                            wait("tensor", CSEM[bf_sem], bf_val)
                for et in range(NE):
                    for tl in range(2):
                        sb_h = et * 256 + tl * 128
                        for c2 in range(len(fcl)):
                            bkk = tl * 2 + c2
                            whi_b = c2 * 2 * NE * 512 + et * 512
                            wlo_b = whi_b + NE * 512
                            op("tensor", lambda e, sb_h=sb_h, bkk=bkk, wb=whi_b,
                               vh=vs_hi, s=(et == 0):
                               e.matmul(pb[bkk][:, :], vh[:, sb_h:sb_h + 128],
                                        S2[:, wb:wb + 512], start=s, stop=False))
                            op("tensor", lambda e, sb_h=sb_h, bkk=bkk, wb=wlo_b,
                               vh=vs_hi:
                               e.matmul(pb[bkk][:, :], vh[:, sb_h:sb_h + 128],
                                        S2[:, wb:wb + 512], start=False,
                                        stop=False))
                            lam = (lambda e, sb_h=sb_h, bkk=bkk, wb=whi_b,
                                   vl=vs_lo, s=(et == NE - 1):
                                   e.matmul(pb[bkk][:, :],
                                            vl[:, sb_h:sb_h + 128],
                                            S2[:, wb:wb + 512], start=False,
                                            stop=s))
                            if et == NE - 1 and tl == 1 and c2 == len(fcl) - 1:
                                mt = topi(lam)
                            else:
                                op("tensor", lam)
                for tl in range(2):
                    for c2 in range(len(fcl)):
                        fc = fcl[c2]
                        bkk = tl * 2 + c2
                        si = bkk
                        wait("scalar", smm, mt)
                        wait_t("scalar", stage_free[si])
                        ev = op("scalar", lambda e, bkk=bkk, si=si:
                                e.copy(out=stage4[:, si * 512:(si + 1) * 512],
                                       in_=pb[bkk][:, :]), ssc)
                        bank_free4[bkk] = (ev, "ssc")
                        wait("scalar", ssc, ev)
                        r0 = ttg * 256 + tl * 128
                        stage_free[si] = stOW.dma(
                            out_d[r0:r0 + 128, fc * 512:(fc + 1) * 512],
                            stage4[:, si * 512:(si + 1) * 512])
                pstate["vs_free_%d" % par2] = smm.c
                prep_step(n_prep_per_iter)
            wo_sweep_done = smm.c
        ph4_mm = smm.c
        prep_step(len(prep_units))
        prep_done_sc = ssc.c
        prep_done_vv = svv.c

        phase_barrier()
        gbuf = [A32[:, 0:512], A32[:, 512:1024]]
        hbuf = [A32[:, 1024:1536], A32[:, 1536:2048]]
        SCGb = A32[:, 2048:2048 + T]
        SCUb = A32[:, 2048 + T:2048 + 2 * T]
        HMb = A32[:, 2048 + 2 * T:2048 + 3 * T]
        WSL = NDT * 512
        wg_sl = [S2[:, 0:WSL], S2[:, WSL:2 * WSL]]
        wu_sl = [S2[:, 2 * WSL:3 * WSL], S2[:, 3 * WSL:4 * WSL]]
        assert 4 * WSL <= S2W and 2048 + 3 * T <= 8192

        op("vector", lambda e: e.memset(gbuf[0][:, 0:128], 1.0), svv)
        op("vector", lambda e: e.memset(HMb[:, 0:T], 0.0), svv)
        bc_bank_cp = [0, 0]
        bc_bank_tr = [0, 0]
        for src, dst in ((tSCG, SCGb), (tSCU, SCUb)):
            for tt in range(NT):
                bk = tt % 2
                pcol = 128 + bk * 128
                if bc_bank_tr[bk]:
                    wait("vector", smm, bc_bank_tr[bk])
                bt = op("vector", lambda e, src=src, tt=tt, pcol=pcol:
                        e.tensor_scalar(out=gbuf[0][:, pcol:pcol + 128],
                                        in0=gbuf[0][:, 0:128],
                                        scalar1=src[:, tt:tt + 1],
                                        scalar2=None, op0=ALU.mult), svv)
                wait("tensor", svv, bt)
                wait("tensor", ssc, bc_bank_cp[bk])
                mt = topi(lambda e, bk=bk, pcol=pcol:
                          e.transpose(pb[bk][:, 0:128],
                                      gbuf[0][:, pcol:pcol + 128], id32[:, :]))
                bc_bank_tr[bk] = mt
                wait("scalar", smm, mt)
                bc_bank_cp[bk] = op("scalar", lambda e, bk=bk, dst=dst, tt=tt:
                                    e.copy(out=dst[:, tt * 128:(tt + 1) * 128],
                                           in_=pb[bk][:, 0:128]), ssc)
        bc_done = ssc.c

        def w8_ap(w8d, fc):
            return bass.AP(w8d, fc * NDT * 128 * 512,
                           [[512, 128], [128 * 512, NDT], [1, 512]])

        wgu_free = [0, 0]
        gu_load_t = {}

        def load_gu(fc):
            if fc >= NFC:
                return
            sl = fc % 2
            wait("sync", smm, wgu_free[sl])
            t1 = stQ8R.dma(wg_sl[sl], w8_ap(wg8_d, fc))
            t2 = stQ8R.dma(wu_sl[sl], w8_ap(wu8_d, fc))
            gu_load_t[fc] = (t1, t2)

        load_gu(0)
        load_gu(1)
        pair_free = [0, 0, 0, 0]
        g_free = [0, 0]
        h_free = [None, None]
        first_sig = False
        itg = 0
        for fc in range(NFC):
            sl = fc % 2
            wait_t("tensor", gu_load_t[fc][0])
            wait_t("tensor", gu_load_t[fc][1])
            if fc == 0:
                wait("tensor", ssc, prep_done_sc)
                wait("tensor", ssc, bc_done)
            for fq in range(4):
                ffd = fc * 4 + fq
                for w in range(NCH):
                    pr = itg % 4
                    wait("tensor", svv, pair_free[pr])
                    for dt in range(NDT):
                        lcol = dt * 512 + fq * 128
                        rcol = dt * T + w * 512
                        op("tensor", lambda e, lcol=lcol, rcol=rcol, pr=pr,
                           sl=sl, s=(dt == 0), p=(dt == NDT - 1):
                           e.matmul(pb[pr * 2][:, :],
                                    wg_sl[sl][:, lcol:lcol + 128],
                                    S1[:, rcol:rcol + 512], start=s, stop=p))
                        lam = (lambda e, lcol=lcol, rcol=rcol, pr=pr, sl=sl,
                               s=(dt == 0), p=(dt == NDT - 1):
                               e.matmul(pb[pr * 2 + 1][:, :],
                                        wu_sl[sl][:, lcol:lcol + 128],
                                        S1[:, rcol:rcol + 512],
                                        start=s, stop=p))
                        if dt == NDT - 1:
                            mt = topi(lam)
                        else:
                            op("tensor", lam)
                    gsl = itg % 2
                    hsl = itg % 2
                    wait("vector", smm, mt)
                    wait_t("vector", h_free[hsl])
                    pg = op("vector", lambda e, pr=pr, hsl=hsl, w=w:
                            e.tensor_tensor(out=hbuf[hsl],
                                            in0=pb[pr * 2][:, :],
                                            in1=SCGb[:, w * 512:(w + 1) * 512],
                                            op=ALU.mult), svv)
                    if not first_sig:
                        wait("scalar", svv, prep_done_vv)
                        first_sig = True
                    wait("scalar", svv, pg)
                    wait("scalar", svv, g_free[gsl])
                    sg = op("scalar", lambda e, gsl=gsl, hsl=hsl:
                            e.activation(out=gbuf[gsl], in_=hbuf[hsl],
                                         func=ACT.Sigmoid), ssc)
                    wait("vector", ssc, sg)
                    op("vector", lambda e, pr=pr, hsl=hsl, w=w:
                       e.tensor_tensor(out=hbuf[hsl], in0=pb[pr * 2 + 1][:, :],
                                       in1=SCUb[:, w * 512:(w + 1) * 512],
                                       op=ALU.mult))
                    pu = op("vector", lambda e, hsl=hsl, gsl=gsl:
                            e.tensor_tensor(out=hbuf[hsl], in0=hbuf[hsl],
                                            in1=gbuf[gsl], op=ALU.mult), svv)
                    pair_free[pr] = pu
                    g_free[gsl] = pu
                    op("vector", lambda e, hsl=hsl, w=w:
                       e.tensor_tensor(out=HMb[:, w * 512:(w + 1) * 512],
                                       in0=HMb[:, w * 512:(w + 1) * 512],
                                       in1=hbuf[hsl], op=ALU.max))
                    op("vector", lambda e, hsl=hsl, w=w:
                       e.scalar_tensor_tensor(
                           out=HMb[:, w * 512:(w + 1) * 512],
                           in0=hbuf[hsl], scalar=-1.0,
                           in1=HMb[:, w * 512:(w + 1) * 512],
                           op0=ALU.mult, op1=ALU.max), svv)
                    wait("scalar", svv, pu)
                    h_free[hsl] = stHW.dma(h_d[ffd, :, w * 512:(w + 1) * 512],
                                           hbuf[hsl])
                    itg += 1
                    if itg % 2 == 0:
                        quant_step(1)
            wgu_free[sl] = smm.c
            load_gu(fc + 2)
        quant_step(len(q_units))
        ph5a_mm = smm.c
        hm_last = svv.c

        for pr in range(4):
            wait("tensor", svv, pair_free[pr])
        wait("tensor", svv, hm_last)
        hm_red = [0, 0]
        for tt in range(NT):
            bk = tt % 2
            wait("tensor", svv, hm_red[bk])
            mt = topi(lambda e, bk=bk, tt=tt:
                      e.transpose(pb[bk][:, 0:128],
                                  HMb[:, tt * 128:(tt + 1) * 128], id32[:, :]))
            wait("vector", smm, mt)
            hm_red[bk] = op("vector", lambda e, bk=bk, tt=tt:
                            e.tensor_reduce(out=tHM[:, tt:tt + 1],
                                            in_=pb[bk][:, 0:128], axis=AX,
                                            op=ALU.max), svv)
        hm_tr_done = smm.c


        op("vector", lambda e: e.tensor_scalar(out=tHM[:, 0:NT], in0=tHM[:, 0:NT],
                                               scalar1=EPS, scalar2=None,
                                               op0=ALU.max))
        op("vector", lambda e: e.reciprocal(out=tSCD[:, 0:NT], in_=tHM[:, 0:NT]))
        op("vector", lambda e: e.tensor_scalar(out=tSCD[:, 0:NT],
                                               in0=tSCD[:, 0:NT],
                                               scalar1=consts[:, 2:3],
                                               scalar2=None, op0=ALU.mult))
        op("vector", lambda e: e.tensor_copy(tRR[:, 0:NT], tSCD[:, 0:NT]))
        hth = op("vector", lambda e: e.tensor_scalar(out=tSCD[:, 0:NT],
                                                     in0=tHM[:, 0:NT],
                                                     scalar1=tS[:, 10:11],
                                                     scalar2=None,
                                                     op0=ALU.mult), svv)

        h5sl = [A32[:, 0:1024], A32[:, 1024:2048]]
        t5b = A32[:, 2048:3072]
        c5b = A32[:, 3072:4096]
        r5b = A32[:, 4096:5120]
        d5b = A32[:, 5120:6144]
        xc5 = A32[:, 6144:8192]
        RRB = A32[:, 8192:8192 + T]
        AHW = NFFD * 256
        buf5 = [S2[:, 0:AHW], S1[:, 0:AHW]]
        wd_sl = [S2[:, AHW:AHW + 4 * D], S2[:, AHW + 4 * D:AHW + 8 * D],
                 S1[:, AHW:AHW + 4 * D]]
        assert AHW + 8 * D <= S2W and AHW + 4 * D <= S1W
        NOC = D // 512
        NST = NT // 2
        NSLAB = NFFD // 4

        wait("vector", ssc, ssc.c)
        op("vector", lambda e: e.memset(gbuf[0][:, 0:128], 1.0), svv)
        rrb_tr = [0, 0]
        rrb_cp = [0, 0]
        for tt in range(NT):
            bk = tt % 2
            pcol = 128 + bk * 128
            if rrb_tr[bk]:
                wait("vector", smm, rrb_tr[bk])
            bt = op("vector", lambda e, tt=tt, pcol=pcol:
                    e.tensor_scalar(out=gbuf[0][:, pcol:pcol + 128],
                                    in0=gbuf[0][:, 0:128],
                                    scalar1=tRR[:, tt:tt + 1],
                                    scalar2=None, op0=ALU.mult), svv)
            wait("tensor", svv, bt)
            wait("tensor", ssc, rrb_cp[bk])
            if tt < 2:
                wait("tensor", svv, hm_red[bk])
            mt = topi(lambda e, bk=bk, pcol=pcol:
                      e.transpose(pb[bk][:, 0:128],
                                  gbuf[0][:, pcol:pcol + 128], id32[:, :]))
            rrb_tr[bk] = mt
            wait("scalar", smm, mt)
            rrb_cp[bk] = op("scalar", lambda e, bk=bk, tt=tt:
                            e.copy(out=RRB[:, tt * 128:(tt + 1) * 128],
                                   in_=pb[bk][:, 0:128]), ssc)
        rrb_done_sc = ssc.c

        wait("sync", svv, svv.c)
        wait("sync", ssc, ssc.c)
        wait("sync", smm, smm.c)
        wait_all("sync", stQ8W)
        wait("scalar", svv, svv.c)
        wait("scalar", smm, smm.c)
        wait_all("scalar", stHW)
        wait_all("scalar", stQ8W)

        bank5_free = [0] * 8
        wd_free = [ph5a_mm] * 3
        wd_load_t = {}
        h5_free = [0, 0]
        evtok = {}
        bst = {"k": 0, "t5_war": 0, "r5_war": 0, "c5_war": 0}
        xc_free = {"t": 0}
        st_free = [None]
        first_ev = {"done": False}
        wd_items = [(sti, gi) for sti in range(NST) for gi in range(NSLAB)]

        def wd_ap(grp):
            return bass.AP(wd8_d, grp * 4 * 128 * D,
                           [[D, 128], [128 * D, 4], [1, D]])

        def load_wd_n(n):
            if n >= len(wd_items):
                return
            sl = n % 3
            wait("sync", smm, wd_free[sl])
            wd_load_t[n] = stQ8R.dma(wd_sl[sl], wd_ap(wd_items[n][1]))

        def build_slab(st, s):
            if st >= NST:
                return
            hs = bst["k"] % 2
            bst["k"] += 1
            hslot = h5sl[hs]
            wait("scalar", svv, h5_free[hs])
            rb = stHR.dma(hslot, bass.AP(h_d, (4 * s) * 128 * T + st * 256,
                                         [[T, 128], [128 * T, 4], [1, 256]]))
            wait_t("vector", rb)
            wait("vector", ssc, bst["t5_war"])
            t1 = 0
            for q in range(4):
                t1 = op("vector", lambda e, q=q, hslot=hslot, st=st:
                        e.tensor_tensor(out=t5b[:, q * 256:(q + 1) * 256],
                                        in0=hslot[:, q * 256:(q + 1) * 256],
                                        in1=RRB[:, st * 256:st * 256 + 256],
                                        op=ALU.mult), svv)
            h5_free[hs] = t1
            wait("scalar", svv, t1)
            t2 = op("scalar", lambda e: e.copy(out=AI[:, 0:1024], in_=t5b), ssc)
            bst["t5_war"] = t2
            wait("scalar", svv, bst["r5_war"])
            t3 = op("scalar", lambda e: e.copy(out=r5b, in_=AI[:, 0:1024]), ssc)
            wait("vector", ssc, t3)
            wait("vector", ssc, bst["c5_war"])
            op("vector", lambda e: e.scalar_tensor_tensor(
                out=c5b, in0=t5b, scalar=-0.5, in1=r5b,
                op0=ALU.add, op1=ALU.is_gt))
            op("vector", lambda e: e.tensor_tensor(out=c5b, in0=r5b, in1=c5b,
                                                   op=ALU.add))
            t6 = op("vector", lambda e: e.scalar_tensor_tensor(
                out=d5b, in0=t5b, scalar=0.5, in1=r5b,
                op0=ALU.add, op1=ALU.is_lt), svv)
            bst["r5_war"] = t6
            t7 = op("vector", lambda e: e.tensor_tensor(out=c5b, in0=c5b,
                                                        in1=d5b,
                                                        op=ALU.subtract), svv)
            wait("scalar", svv, t7)
            t9 = op("scalar", lambda e, st=st, s=s:
                    e.copy(out=buf5[st % 2][:, s * 1024:(s + 1) * 1024],
                           in_=c5b), ssc)
            bst["c5_war"] = t9
            evtok[(st, s)] = t9

        def sweep_grp(st, g):
            n = st * NSLAB + g
            sl = n % 3
            wait("tensor", ssc, evtok[(st, g)])
            wait_t("tensor", wd_load_t[n])
            if g == 0:
                if st == 0:
                    wait("tensor", ssc, rrb_done_sc)
                for bkk in range(2 * NOC):
                    wait("tensor", svv, bank5_free[bkk])
            for sub in range(4):
                ffd = g * 4 + sub
                for tl in range(2):
                    lcol = ffd * 256 + tl * 128
                    for oc in range(NOC):
                        lam = (lambda e, lcol=lcol, tl=tl, oc=oc, sl=sl, st=st,
                               sub=sub, s=(ffd == 0), p=(ffd == NFFD - 1):
                               e.matmul(pb[tl * NOC + oc][:, :],
                                        buf5[st % 2][:, lcol:lcol + 128],
                                        wd_sl[sl][:, sub * D + oc * 512:
                                                  sub * D + oc * 512 + 512],
                                        start=s, stop=p))
                        if sub == 3 and tl == 1 and oc == NOC - 1:
                            wd_free[sl] = topi(lam)
                        else:
                            op("tensor", lam)

        def evac(st, sweep_mt):
            for tl in range(2):
                tt = st * 2 + tl
                wait("sync", svv, xc_free["t"])
                if not first_ev["done"]:
                    wait("sync", smm, hm_tr_done)
                    first_ev["done"] = True
                wait_t("sync", st_free[0])
                xt = stX.dma(xc5[:, 0:D], xm_d[tt * 128:(tt + 1) * 128, 0:D])
                wait("vector", smm, sweep_mt)
                wait_t("vector", xt)
                wait("vector", svv, hth)
                evt = 0
                for oc in range(NOC):
                    bkk = tl * NOC + oc
                    evt = op("vector", lambda e, bkk=bkk, oc=oc, tt=tt:
                             e.scalar_tensor_tensor(
                                 out=xc5[:, oc * 512:(oc + 1) * 512],
                                 in0=pb[bkk][:, :],
                                 scalar=tSCD[:, tt:tt + 1],
                                 in1=xc5[:, oc * 512:(oc + 1) * 512],
                                 op0=ALU.mult, op1=ALU.add), svv)
                    bank5_free[bkk] = evt
                xc_free["t"] = evt
                wait("gpsimd", svv, evt)
                st_free[0] = stOA.dma(out_d[tt * 128:(tt + 1) * 128, 0:D],
                                      xc5[:, 0:D], accum_op=ALU.add)

        SERIAL5B = False
        if SERIAL5B:
            for n0 in range(3):
                load_wd_n(n0)
            for st in range(NST):
                for s in range(NSLAB):
                    build_slab(st, s)
                for k in range(NSLAB):
                    sweep_grp(st, k)
                    load_wd_n(st * NSLAB + k + 3)
                evac(st, smm.c)
        else:
            for n0 in range(3):
                load_wd_n(n0)
            for s in range(NSLAB):
                build_slab(0, s)
            for st in range(NST):
                for k in range(NSLAB):
                    build_slab(st + 1, k)
                    sweep_grp(st, k)
                    load_wd_n(st * NSLAB + k + 3)
                evac(st, smm.c)

        wait_all("gpsimd", stOA)
        wait_all("gpsimd", stOW)
        wait_all("sync", stOA)

        with nc.Block() as block:
            @block.sync
            def _(e):
                for f in S["sync"]:
                    f(e)

            @block.tensor
            def _(e):
                for f in S["tensor"]:
                    f(e)

            @block.vector
            def _(e):
                for f in S["vector"]:
                    f(e)

            @block.scalar
            def _(e):
                for f in S["scalar"]:
                    f(e)

            @block.gpsimd
            def _(e):
                for f in S["gpsimd"]:
                    f(e)

    return nc


_PROG = {}


def _get_prog(cfg_key):
    if cfg_key not in _PROG:
        _PROG[cfg_key] = build_program(dict(zip(("D", "T", "FF"), cfg_key)))
    return _PROG[cfg_key]


def make_host_inputs(cfg, Wq, Wk, Wv, Wo, Wg, Wu, Wdn):
    D, T, FF = cfg["D"], cfg["T"], cfg["FF"]
    NE = D // 128
    NDT = D // 128
    NFFD = FF // 128

    def att_img(W):
        a = np.ascontiguousarray(W.T).reshape(NDT, 128, NE, 128)
        return np.ascontiguousarray(a.transpose(2, 1, 0, 3).reshape(NE, 128, D))

    def mov_img(W, nblk):
        wt = np.ascontiguousarray(W.T)
        return np.ascontiguousarray(wt.reshape(nblk, 128, wt.shape[1]))

    d = {}
    d["wq"] = att_img(Wq)
    d["wk"] = att_img(Wk)
    d["wv"] = att_img(Wv)
    NFCW = D // 512
    wot = np.ascontiguousarray(Wo.T).reshape(NE, 128, NFCW, 512)
    d["wo"] = np.ascontiguousarray(
        wot.transpose(2, 1, 0, 3).reshape(NFCW, 128, NE * 512))
    d["wg"] = mov_img(Wg, NDT)
    d["wu"] = mov_img(Wu, NDT)
    d["wd"] = mov_img(Wdn, NFFD)
    d["id32"] = np.eye(128, dtype=np.float32)
    d["ones16"] = np.ones((128, 2), dtype=ml_dtypes.bfloat16)
    d["onesr16"] = np.ones((1, 128), dtype=ml_dtypes.bfloat16)
    consts = np.zeros((128, 8), dtype=np.float32)
    consts[:, 0] = EPS
    consts[:, 1] = 1.0 / D
    consts[:, 2] = 127.0
    consts[:, 3] = float(D)
    consts[:, 4] = 1.0
    d["consts"] = consts
    return d


def kernel(L0_attn_x, L0_attn_Wq, L0_attn_Wk, L0_attn_Wv, L0_attn_Wo,
           L0_ffn_w_gate, L0_ffn_w_up, L0_ffn_w_down):
    cfg = full_cfg()
    D, T, FF = cfg["D"], cfg["T"], cfg["FF"]
    x = np.ascontiguousarray(np.asarray(L0_attn_x, dtype=np.float32))
    B_, S_, D_ = x.shape
    assert D_ == D and S_ == 2 * T

    shared = make_host_inputs(
        cfg,
        np.asarray(L0_attn_Wq, np.float32), np.asarray(L0_attn_Wk, np.float32),
        np.asarray(L0_attn_Wv, np.float32), np.asarray(L0_attn_Wo, np.float32),
        np.asarray(L0_ffn_w_gate, np.float32),
        np.asarray(L0_ffn_w_up, np.float32),
        np.asarray(L0_ffn_w_down, np.float32))

    zeros_pref = np.zeros((T, D), dtype=np.float32)
    in_maps = []
    for c in range(8):
        b, h = c // 2, c % 2
        m = dict(shared)
        m["xm"] = np.ascontiguousarray(x[b, h * T:(h + 1) * T])
        m["xp"] = np.ascontiguousarray(x[b, 0:T]) if h == 1 else zeros_pref
        in_maps.append(m)

    nc_prog = _get_prog((D, T, FF))
    res = run_bass_kernel_spmd(nc_prog, in_maps, core_ids=list(range(8)))
    out = np.empty((B_, S_, D), dtype=np.float32)
    for c in range(8):
        b, h = c // 2, c % 2
        out[b, h * T:(h + 1) * T] = np.asarray(res.results[c]["out"])
    return out


# version 16
# speedup vs baseline: 1.0339x; 1.0339x over previous
import numpy as np
import ml_dtypes
from contextlib import ExitStack

import concourse.bass as bass
import concourse.mybir as mybir
from concourse.bass_utils import run_bass_kernel_spmd

F32 = mybir.dt.float32
BF16 = mybir.dt.bfloat16
FP8 = mybir.dt.float8e4
I32 = mybir.dt.int32
AX = mybir.AxisListType.X
ALU = mybir.AluOpType
ACT = mybir.ActivationFunctionType
EPS = 1e-5

ENGS = ("sync", "tensor", "vector", "scalar", "gpsimd")


def full_cfg():
    return dict(D=2048, T=2048, FF=8192)


class Sem:
    __slots__ = ("h", "c")

    def __init__(self, stack, nc, name):
        self.h = stack.enter_context(nc.semaphore(name))
        self.c = 0

    def bump(self, n):
        self.c += n
        return self.c


def build_program(cfg):
    D, T, FF = cfg["D"], cfg["T"], cfg["FF"]
    NT = T // 128
    NE = D // 128
    NDT = D // 128
    NFFD = FF // 128
    NFC = FF // 512
    NCH = T // 512
    QW = 1024 if (D % 1024 == 0 and FF % 1024 == 0) else 512
    NWINV = 1.0 / (D * FF)
    SEQ = 2 * T
    assert NT % 2 == 0 and NCH >= 1 and D % QW == 0 and FF % QW == 0

    nc = bass.Bass("TRN2", target_bir_lowering=False)

    xm_d = nc.dram_tensor("xm", [T, D], F32, kind="ExternalInput")
    xp_d = nc.dram_tensor("xp", [T, D], F32, kind="ExternalInput")
    wq_d = nc.dram_tensor("wq", [NE, 128, D], F32, kind="ExternalInput")
    wk_d = nc.dram_tensor("wk", [NE, 128, D], F32, kind="ExternalInput")
    wv_d = nc.dram_tensor("wv", [NE, 128, D], F32, kind="ExternalInput")
    wo_d = nc.dram_tensor("wo", [D // 512, 128, NE * 512], F32, kind="ExternalInput")
    wg_d = nc.dram_tensor("wg", [NDT, 128, FF], F32, kind="ExternalInput")
    wu_d = nc.dram_tensor("wu", [NDT, 128, FF], F32, kind="ExternalInput")
    wd_d = nc.dram_tensor("wd", [NFFD, 128, D], F32, kind="ExternalInput")
    id32_d = nc.dram_tensor("id32", [128, 128], F32, kind="ExternalInput")
    ones_d = nc.dram_tensor("ones16", [128, 2], BF16, kind="ExternalInput")
    onesr_d = nc.dram_tensor("onesr16", [1, 128], BF16, kind="ExternalInput")
    consts_d = nc.dram_tensor("consts", [128, 8], F32, kind="ExternalInput")
    out_d = nc.dram_tensor("out", [T, D], F32, kind="ExternalOutput")

    vhat_d = nc.dram_tensor("vhat_s", [2, NE, 128, T], BF16)
    h_d = nc.dram_tensor("h_s", [NFFD, 128, T], F32)
    rrb_d = nc.dram_tensor("rrb_s", [128, T], F32)
    wg8_d = nc.dram_tensor("wg8_s", [NFC, NDT, 128, 512], BF16)
    wu8_d = nc.dram_tensor("wu8_s", [NFC, NDT, 128, 512], BF16)
    wd8_d = nc.dram_tensor("wd8_s", [NFFD, 128, D], BF16)

    with ExitStack() as stack:
        sb = lambda nm, shp, dt=F32: stack.enter_context(nc.sbuf_tensor(nm, shp, dt))

        S1W = max(NDT * T, NE * D, NFFD * 256 + 4 * D)
        S2W = max(NDT * T, NE * D, NFFD * 256 + 2 * 4 * D, 4 * NDT * 512)
        A32 = sb("A32", [128, 10240])
        S1 = sb("S1", [128, S1W], BF16)
        S2 = sb("S2", [128, S2W], BF16)
        A16 = sb("A16", [128, 16384], BF16)
        AI = sb("AI", [128, max(QW, 1024)], I32)
        id32 = sb("id32s", [128, 128])
        ones = sb("ones16s", [128, 2], BF16)
        onesr = sb("onesr16s", [1, 128], BF16)
        consts = sb("constss", [128, 8])
        nc.const_aps.register(F32, EPS, consts[:, 0:1])
        nc.const_aps.register(F32, 1.0 / D, consts[:, 1:2])
        nc.const_aps.register(F32, 1.0, consts[:, 4:5])

        tC = sb("tC", [128, NE])
        tM = sb("tM", [128, NT])
        tNM = sb("tNM", [128, NT])
        tR = sb("tR", [128, NT])
        tRM = sb("tRM", [128, NT])
        tRR = sb("tRR", [128, NT])
        tSCG = sb("tSCG", [128, NT])
        tSCU = sb("tSCU", [128, NT])
        tSCD = sb("tSCD", [128, NT])
        tHM = sb("tHM", [128, NT])
        tS = sb("tS", [128, 12])
        tSn = sb("tSn", [128, 4])
        tACC = sb("tACC", [128, 4])
        tT1 = sb("tT1", [128, 8])
        tT2 = sb("tT2", [128, 8])
        tSum = sb("tSum", [1, 4])
        tSplit = sb("tSplit", [128, 4], BF16)

        pb = [stack.enter_context(nc.psum_tensor(f"pb{i}", [128, 512], F32))
              for i in range(8)]

        ALL_SEMS = []

        def mksem(name):
            s = Sem(stack, nc, name)
            ALL_SEMS.append(s)
            return s

        smm = mksem("smm")
        ssc = mksem("ssc")
        svv = mksem("svv")
        sgc = mksem("sgc")
        CSEM = {"smm": smm, "ssc": ssc, "svv": svv, "sgc": sgc}

        S = {k: [] for k in ENGS}

        def em(eng, fn):
            S[eng].append(fn)

        def wait(eng, sem, val):
            if val and val > 0:
                em(eng, lambda e, s=sem.h, v=val: e.wait_ge(s, v))

        def wait_t(eng, tok):
            if tok:
                wait(eng, tok[0], tok[1])

        SELF = {"vector": svv, "scalar": ssc, "gpsimd": sgc}

        def op(eng, fn, sem=None, n=1):
            if eng in SELF:
                s = SELF[eng]
                assert sem is None or sem is s
                wait(eng, s, s.c)
                t = s.bump(1)
                em(eng, lambda e, f=fn, sh=s.h: f(e).then_inc(sh, 1))
                return t
            if sem is None:
                em(eng, fn)
                return None
            t = sem.bump(n)
            em(eng, lambda e, f=fn, s=sem.h, k=n: f(e).then_inc(s, k))
            return t

        def topi(fn):
            t = smm.bump(1)
            em("tensor", lambda e, f=fn, sh=smm.h: f(e).then_inc(sh, 1))
            return t

        class DS:

            def __init__(self, name, eng, nsem=2):
                self.sems = [mksem(f"{name}{i}") for i in range(nsem)]
                self.eng = eng
                self.i = 0

            def dma(self, out, in_, **kw):
                s = self.sems[self.i % len(self.sems)]
                self.i += 1
                wait(self.eng, s, s.c)
                t = s.bump(16)
                em(self.eng, lambda e, o=out, i_=in_, kw=dict(kw), sh=s.h:
                   e.dma_start(out=o, in_=i_, **kw).then_inc(sh, 16))
                return (s, t)

            def all_toks(self):
                return [(s, s.c) for s in self.sems]

        def wait_all(eng, stream):
            for tok in stream.all_toks():
                wait_t(eng, tok)

        stX = DS("sx", "sync", 3)
        stWA = DS("swa", "sync", 3)
        stWM = DS("swm", "sync", 3)
        stWQ = DS("swq", "sync", 3)
        stWO = DS("swo", "sync", 2)
        stVR = DS("svr", "sync", 4)
        stHR = DS("shr", "scalar", 3)
        stVW = DS("svw", "scalar", 2)
        stHW = DS("shw", "scalar", 2)
        stOW = DS("sow", "scalar", 2)
        stQ8W = DS("sq8w", "scalar", 2)
        stQ8R = DS("sq8r", "sync", 3)
        stOA = DS("soa", "gpsimd", 2)

        def phase_barrier():
            vals = [(s, s.c) for s in ALL_SEMS]
            for eng in ENGS:
                for s_, v_ in vals:
                    wait(eng, s_, v_)

        for o, i in ((id32[:, :], id32_d[:, :]), (ones[:, :], ones_d[:, :]),
                     (onesr[:, :], onesr_d[:, :]), (consts[:, :], consts_d[:, :])):
            stX.dma(o, i)
        phase_barrier()

        wimgs = [(wg_d, NDT, FF, 0), (wu_d, NDT, FF, 1), (wd_d, NFFD, D, 2)]
        mean_units = []
        for wdd, nblk, blkw, wi in wimgs:
            for bi in range(nblk):
                for c in range(blkw // QW):
                    mean_units.append((wdd[bi, :, c * QW:(c + 1) * QW], wi))
        wm_slot = [A32[:, 6144:6144 + QW], A32[:, 7168:7168 + QW],
                   A32[:, 8192:8192 + QW], A32[:, 9216:9216 + QW]]
        mu = {"i": 0, "load_t": {}, "free_t": {}}

        def mean_issue_load(idx):
            if idx >= len(mean_units):
                return
            src, _ = mean_units[idx]
            slot = idx % 4
            ft = mu["free_t"].get(slot)
            if ft:
                wait("sync", svv, ft)
            mu["load_t"][idx] = stWM.dma(wm_slot[slot], src)

        def mean_step(nunits):
            for _ in range(nunits):
                i = mu["i"]
                if i >= len(mean_units):
                    return
                if i == 0:
                    for k_ in range(4):
                        mean_issue_load(k_)
                _, wi = mean_units[i]
                slot = i % 4
                wait_t("vector", mu["load_t"][i])
                op("vector", lambda e, s=wm_slot[slot]:
                   e.tensor_reduce(out=tT1[:, 0:1], in_=s, axis=AX, op=ALU.add,
                                   apply_absolute_value=True))
                ft = op("vector", lambda e, wi=wi:
                        e.tensor_tensor(out=tACC[:, wi:wi + 1],
                                        in0=tACC[:, wi:wi + 1],
                                        in1=tT1[:, 0:1], op=ALU.add), svv)
                mu["free_t"][slot] = ft
                mean_issue_load(i + 4)
                mu["i"] += 1

        op("vector", lambda e: e.memset(tACC[:, 0:4], 0.0), svv)

        q_units = []
        for wdd, nblk, blkw, wi in wimgs:
            for bi in range(nblk):
                for c in range(blkw // QW):
                    src = wdd[bi, :, c * QW:(c + 1) * QW]
                    if wi < 2:
                        w8 = wg8_d if wi == 0 else wu8_d
                        dsts = [w8[(c * QW) // 512 + j, bi, :, :]
                                for j in range(QW // 512)]
                    else:
                        dsts = [wd8_d[bi, :, c * QW:(c + 1) * QW]]
                    q_units.append((src, wi, dsts))
        q_in_sl = [A32[:, 8192:8192 + QW], A32[:, 9216:9216 + QW]]
        qp_buf = A16[:, 14336:14336 + QW]
        qn_buf = A16[:, 15360:15360 + QW]
        qb_buf = A16[:, 12288:12288 + QW]
        qu = {"i": 0, "wq_free": None, "gu_done": None, "th_ready": 0}

        def quant_step(nunits):
            for _ in range(nunits):
                i = qu["i"]
                if i >= len(q_units):
                    return
                src, wi, dsts = q_units[i]
                q_in = q_in_sl[i % 2]
                wait("sync", svv, qu.get("b_done_%d" % (i % 2), 0))
                lt = stWQ.dma(q_in, src)
                wait_t("vector", lt)
                op("vector", lambda e, wi=wi, q_in=q_in:
                   e.tensor_scalar(out=qp_buf, in0=q_in,
                                   scalar1=tS[:, 3 + wi:4 + wi], scalar2=None,
                                   op0=ALU.is_gt))
                nt_ = op("vector", lambda e, wi=wi, q_in=q_in:
                         e.tensor_scalar(out=qn_buf, in0=q_in,
                                         scalar1=tSn[:, wi:wi + 1], scalar2=None,
                                         op0=ALU.is_lt))
                qu["b_done_%d" % (i % 2)] = nt_
                for tok_ in (qu["wq_free"] or []):
                    wait_t("vector", tok_)
                bt = op("vector", lambda e:
                        e.tensor_tensor(out=qb_buf, in0=qp_buf, in1=qn_buf,
                                        op=ALU.subtract), svv)

                wait("scalar", svv, bt)
                nj = len(dsts)
                w = QW // nj
                wts = []
                for j, dst in enumerate(dsts):
                    wts.append(stQ8W.dma(dst, qb_buf[:, j * w:(j + 1) * w]))
                qu["wq_free"] = wts
                if wi < 2:
                    qu["gu_done"] = wts[-1]
                qu["i"] += 1

        xslot = [A32[:, 0:2048], A32[:, 2048:4096]]
        xnslot = [A32[:, 4096:6144], A32[:, 6144:8192]]
        ln = {"xfree": [[0, 0], [0, 0]], "pair_tr": 0, "bankf": [0, 0]}

        def ln_phase(xd, save_stats):
            for pair in range(NT // 2):
                vts = {}
                sts = {}
                xts = {}
                for half in range(2):
                    i = pair * 2 + half
                    sl = half
                    wait("sync", svv, ln["xfree"][sl][0])
                    wait("sync", ssc, ln["xfree"][sl][1])
                    xt = stX.dma(xslot[sl][:, 0:D], xd[i * 128:(i + 1) * 128, 0:D])
                    xts[i] = xt
                    wait_t("vector", xt)
                    op("vector", lambda e, sl=sl:
                       e.reduce_sum(out=tT1[:, 0:1], in_=xslot[sl][:, 0:D], axis=AX))
                    op("vector", lambda e, i=i:
                       e.tensor_scalar(out=tM[:, i:i + 1], in0=tT1[:, 0:1],
                                       scalar1=1.0 / D, scalar2=None, op0=ALU.mult))
                    vt = op("vector", lambda e, i=i:
                            e.tensor_scalar(out=tNM[:, i:i + 1], in0=tT1[:, 0:1],
                                            scalar1=-1.0 / D, scalar2=None,
                                            op0=ALU.mult), svv)
                    vts[i] = vt
                    wait("scalar", svv, vt)
                    wait("scalar", smm, ln["pair_tr"])
                    wait_t("scalar", xt)
                    op("scalar", lambda e, sl=sl, i=i:
                       e.activation(out=xnslot[sl][:, 0:D], in_=xslot[sl][:, 0:D],
                                    func=ACT.Square, bias=tNM[:, i:i + 1],
                                    accum_out=tT1[:, 1 + sl:2 + sl]))
                    st_ = op("scalar", lambda e, sl=sl:
                             e.activation(out=tT2[:, 4 + sl:5 + sl],
                                          in_=tT1[:, 1 + sl:2 + sl],
                                          func=ACT.Sqrt, scale=1.0 / D, bias=EPS),
                             ssc)
                    sts[i] = st_
                for half in range(2):
                    i = pair * 2 + half
                    sl = half
                    wait("vector", ssc, sts[i])
                    wait("vector", smm, ln["pair_tr"])
                    op("vector", lambda e, i=i, sl=sl:
                       e.reciprocal(out=tR[:, i:i + 1], in_=tT2[:, 4 + sl:5 + sl]))
                    xnt = op("vector", lambda e, sl=sl, i=i:
                             e.tensor_scalar(out=xnslot[sl][:, 0:D],
                                             in0=xslot[sl][:, 0:D],
                                             scalar1=tM[:, i:i + 1],
                                             scalar2=tR[:, i:i + 1],
                                             op0=ALU.subtract, op1=ALU.mult), svv)
                    if save_stats:
                        op("vector", lambda e, sl=sl, i=i:
                           e.tensor_reduce(out=tRM[:, i:i + 1],
                                           in_=xnslot[sl][:, 0:D], axis=AX,
                                           op=ALU.max, apply_absolute_value=True))
                    ln["xfree"][sl] = [xnt, sts[i]]
                xn_ready = svv.c
                wait("tensor", svv, xn_ready)
                for f in range(NE):
                    bk = 4 + (f % 2)
                    wait("tensor", svv, ln["bankf"][f % 2])
                    op("tensor", lambda e, bk=bk, f=f:
                       e.transpose(pb[bk][:, 0:128],
                                   xnslot[0][:, f * 128:(f + 1) * 128],
                                   id32[:, :]))
                    mt = topi(lambda e, bk=bk, f=f:
                              e.transpose(pb[bk][:, 128:256],
                                          xnslot[1][:, f * 128:(f + 1) * 128],
                                          id32[:, :]))
                    wait("scalar", smm, mt)
                    base = f * T + pair * 256
                    ht = op("scalar", lambda e, bk=bk, base=base:
                            e.copy(out=S1[:, base:base + 256],
                                   in_=pb[bk][:, 0:256]), ssc)
                    wait("vector", ssc, ht)
                    lt_ = op("vector", lambda e, bk=bk, base=base:
                             e.scalar_tensor_tensor(out=S2[:, base:base + 256],
                                                    in0=pb[bk][:, 0:256], scalar=0.0,
                                                    in1=S1[:, base:base + 256],
                                                    op0=ALU.add, op1=ALU.subtract),
                             svv)
                    ln["bankf"][f % 2] = lt_
                ln["pair_tr"] = smm.c

        ln_phase(xp_d, False)

        phase_barrier()
        w_in = A32[:, 0:2048]
        kbuf = A32[:, 2048:4096]
        kvbuf = A32[:, 4096:6144]
        whi = [A16[:, 0:2048], A16[:, 2048:4096]]
        wlo = [A16[:, 4096:6144], A16[:, 6144:8192]]
        vh_hi = A16[:, 8192:8192 + 2048]
        vh_lo = A16[:, 10240:10240 + 2048]

        sp = {"w_free": 0, "whilo_free": [0, 0]}

        def stream_weight_split(wdram, e, par):
            wait("sync", svv, sp["w_free"])
            lt = stWA.dma(w_in[:, 0:D], wdram[e, :, :])
            wait("vector", smm, sp["whilo_free"][par])
            wait_t("vector", lt)
            op("vector", lambda e_, p=par:
               e_.tensor_copy(whi[p][:, 0:D], w_in[:, 0:D]))
            st = op("vector", lambda e_, p=par:
                    e_.scalar_tensor_tensor(out=wlo[p][:, 0:D], in0=w_in[:, 0:D],
                                            scalar=0.0, in1=whi[p][:, 0:D],
                                            op0=ALU.add, op1=ALU.subtract), svv)
            sp["w_free"] = st
            return st

        def proj_mms(par, bank_free, split_t):
            res = []
            wait("tensor", svv, split_t)
            for c in range(NCH):
                bf_val, bf_sem = bank_free[c]
                if bf_val:
                    wait("tensor", CSEM[bf_sem], bf_val)
                for dt in range(NDT):
                    mv = dt * T + c * 512
                    op("tensor", lambda e, c=c, dt=dt, p=par, mv=mv, s=(dt == 0):
                       e.matmul(pb[c][:, :], whi[p][:, dt * 128:(dt + 1) * 128],
                                S1[:, mv:mv + 512], start=s, stop=False))
                    op("tensor", lambda e, c=c, dt=dt, p=par, mv=mv:
                       e.matmul(pb[c][:, :], whi[p][:, dt * 128:(dt + 1) * 128],
                                S2[:, mv:mv + 512], start=False, stop=False))
                    lam = (lambda e, c=c, dt=dt, p=par, mv=mv,
                           s=(dt == NDT - 1):
                           e.matmul(pb[c][:, :],
                                    wlo[p][:, dt * 128:(dt + 1) * 128],
                                    S1[:, mv:mv + 512], start=False, stop=s))
                    if dt == NDT - 1:
                        last_t = topi(lam)
                    else:
                        op("tensor", lam)
                res.append((c, last_t))
            sp["whilo_free"][par] = smm.c
            return res

        bank_free = [(0, "svv")] * NCH
        next_k = next_v = None
        for e in range(NE):
            st_k = next_k if next_k is not None else \
                stream_weight_split(wk_d, e, 0)
            st_v = next_v if next_v is not None else \
                stream_weight_split(wv_d, e, 1)
            kres = proj_mms(0, bank_free, st_k)
            kev = []
            for c, mt in kres:
                wait("scalar", smm, mt)
                kev.append(op("scalar", lambda e_, c=c:
                              e_.copy(out=kbuf[:, c * 512:(c + 1) * 512],
                                      in_=pb[c][:, :]), ssc))
            vres = proj_mms(1, [(kev[c], "ssc") for c in range(NCH)], st_v)
            kvts = []
            for c, mt in vres:
                wait("vector", smm, mt)
                wait("vector", ssc, kev[c])
                kvts.append(op("vector", lambda e_, c=c:
                              e_.tensor_tensor(out=kvbuf[:, c * 512:(c + 1) * 512],
                                               in0=kbuf[:, c * 512:(c + 1) * 512],
                                               in1=pb[c][:, :], op=ALU.mult), svv))
            bank_free = [(kvts[c], "svv") for c in range(NCH)]
            op("vector", lambda e_, e=e:
               e_.tensor_reduce(out=tC[:, e:e + 1], in_=kvbuf[:, 0:T], axis=AX,
                                op=ALU.add), svv)
            if e + 1 < NE:
                next_k = stream_weight_split(wk_d, e + 1, 0)
                next_v = stream_weight_split(wv_d, e + 1, 1)
            mean_step(len(mean_units) // NE + 1)
        mean_step(len(mean_units))

        ph1_mm = smm.c
        phase_barrier()
        ln["pair_tr"] = ph1_mm
        ln["xfree"] = [[svv.c, ssc.c], [svv.c, ssc.c]]
        ln_phase(xm_d, True)

        op("vector", lambda e: e.tensor_scalar(out=tRM[:, 0:NT], in0=tRM[:, 0:NT],
                                               scalar1=EPS, scalar2=None,
                                               op0=ALU.max))
        op("vector", lambda e: e.reciprocal(out=tT2[:, 0:1], in_=consts[:, 2:3]))
        op("vector", lambda e: e.reciprocal(out=tRR[:, 0:NT], in_=tRM[:, 0:NT]))
        op("vector", lambda e: e.tensor_scalar(out=tRR[:, 0:NT], in0=tRR[:, 0:NT],
                                               scalar1=consts[:, 2:3], scalar2=None,
                                               op0=ALU.mult))
        op("vector", lambda e: e.tensor_tensor(out=tRR[:, 0:NT], in0=tRR[:, 0:NT],
                                               in1=tR[:, 0:NT], op=ALU.mult))
        sden = op("scalar", lambda e: e.activation(out=tT2[:, 1:2],
                                                   in_=consts[:, 3:4],
                                                   func=ACT.Sqrt))
        wait("vector", ssc, sden)
        op("vector", lambda e: e.tensor_scalar(out=tS[:, 6:7], in0=tT2[:, 1:2],
                                               scalar1=1.0 / SEQ, scalar2=None,
                                               op0=ALU.mult), svv)

        def split3(src, prow=slice(0, 128)):
            op("vector", lambda e: e.tensor_copy(tSplit[prow, 0:1], src))
            op("vector", lambda e: e.scalar_tensor_tensor(
                out=tT2[prow, 2:3], in0=src, scalar=0.0, in1=tSplit[prow, 0:1],
                op0=ALU.add, op1=ALU.subtract))
            op("vector", lambda e: e.tensor_copy(tSplit[prow, 1:2], tT2[prow, 2:3]))
            op("vector", lambda e: e.scalar_tensor_tensor(
                out=tT2[prow, 3:4], in0=tT2[prow, 2:3], scalar=0.0,
                in1=tSplit[prow, 1:2], op0=ALU.add, op1=ALU.subtract))
            return op("vector", lambda e: e.tensor_copy(tSplit[prow, 2:3],
                                                        tT2[prow, 3:4]), svv)

        for wi in range(3):
            st3 = split3(tACC[:, wi:wi + 1])
            wait("tensor", svv, st3)
            for j in range(2):
                op("tensor", lambda e, j=j: e.matmul(
                    pb[4][0:1, 0:1], ones[:, 0:1], tSplit[:, j:j + 1],
                    start=(j == 0), stop=False))
            mt = topi(lambda e: e.matmul(
                pb[4][0:1, 0:1], ones[:, 0:1], tSplit[:, 2:3],
                start=False, stop=True))
            wait("scalar", smm, mt)
            sc1 = op("scalar", lambda e, wi=wi:
                     e.copy(out=tSum[0:1, wi:wi + 1], in_=pb[4][0:1, 0:1]), ssc)
            wait("vector", ssc, sc1)
            op("vector", lambda e, wi=wi: e.tensor_scalar(
                out=tSum[0:1, wi:wi + 1], in0=tSum[0:1, wi:wi + 1],
                scalar1=NWINV, scalar2=EPS, op0=ALU.mult, op1=ALU.add))
            st3b = split3(tSum[0:1, wi:wi + 1], prow=slice(0, 1))
            wait("tensor", svv, st3b)
            for j in range(2):
                op("tensor", lambda e, j=j: e.matmul(
                    pb[5][:, 0:1], onesr[:, :], tSplit[0:1, j:j + 1],
                    start=(j == 0), stop=False))
            mt2 = topi(lambda e: e.matmul(
                pb[5][:, 0:1], onesr[:, :], tSplit[0:1, 2:3],
                start=False, stop=True))
            wait("scalar", smm, mt2)
            sc2 = op("scalar", lambda e, wi=wi:
                     e.copy(out=tS[:, wi:wi + 1], in_=pb[5][:, 0:1]), ssc)
            wait("vector", ssc, sc2)
            op("vector", lambda e, wi=wi: e.tensor_scalar(
                out=tS[:, 3 + wi:4 + wi], in0=tS[:, wi:wi + 1], scalar1=0.5,
                scalar2=None, op0=ALU.mult))
            op("vector", lambda e, wi=wi: e.tensor_scalar(
                out=tSn[:, wi:wi + 1], in0=tS[:, wi:wi + 1], scalar1=-0.5,
                scalar2=None, op0=ALU.mult))
            op("vector", lambda e, wi=wi: e.tensor_tensor(
                out=tS[:, 8 + wi:9 + wi], in0=tS[:, wi:wi + 1], in1=tT2[:, 0:1],
                op=ALU.mult), svv)
        qu["th_ready"] = svv.c
        op("vector", lambda e: e.tensor_scalar(out=tSCG[:, 0:NT], in0=tRM[:, 0:NT],
                                               scalar1=tS[:, 8:9], scalar2=None,
                                               op0=ALU.mult))
        op("vector", lambda e: e.tensor_scalar(out=tSCU[:, 0:NT], in0=tRM[:, 0:NT],
                                               scalar1=tS[:, 9:10], scalar2=None,
                                               op0=ALU.mult))
        op("vector", lambda e: e.memset(tHM[:, 0:NT], 0.0), svv)

        phase_barrier()
        qbuf = A32[:, 2048:4096]
        kbuf3 = A32[:, 4096:6144]
        kvbuf3 = A32[:, 6144:8192]

        vhat_free = None
        q_consumed = 0
        k_consumed = 0
        bank_free = [(svv.c, "svv")] * NCH
        next_split = None
        next_ksplit = None
        for e in range(NE):
            st_q = next_split if next_split is not None else \
                stream_weight_split(wq_d, e, 0)
            qres = proj_mms(0, bank_free, st_q)
            qev = []
            for c, mt in qres:
                wait("scalar", smm, mt)
                if c == 0:
                    wait("scalar", svv, q_consumed)
                qev.append(op("scalar", lambda e_, c=c:
                              e_.copy(out=qbuf[:, c * 512:(c + 1) * 512],
                                      in_=pb[c][:, :]), ssc))
            st_k = next_ksplit if next_ksplit is not None else \
                stream_weight_split(wk_d, e, 1)
            kres = proj_mms(1, [(qev[c], "ssc") for c in range(NCH)], st_k)
            kev = []
            for c, mt in kres:
                wait("scalar", smm, mt)
                if c == 0:
                    wait("scalar", svv, k_consumed)
                kev.append(op("scalar", lambda e_, c=c:
                              e_.copy(out=kbuf3[:, c * 512:(c + 1) * 512],
                                      in_=pb[c][:, :]), ssc))
            st_v = stream_weight_split(wv_d, e, 0)
            vres = proj_mms(0, [(kev[c], "ssc") for c in range(NCH)], st_v)
            if e + 1 < NE:
                next_split = stream_weight_split(wq_d, e + 1, 0)
                next_ksplit = stream_weight_split(wk_d, e + 1, 1)
            kvts = []
            for c, mt in vres:
                wait("vector", smm, mt)
                wait("vector", ssc, kev[c])
                if c == 0 and vhat_free is not None:
                    wait_t("vector", vhat_free[0])
                    wait_t("vector", vhat_free[1])
                kvts.append(op("vector", lambda e_, c=c:
                              e_.tensor_tensor(out=kvbuf3[:, c * 512:(c + 1) * 512],
                                               in0=kbuf3[:, c * 512:(c + 1) * 512],
                                               in1=pb[c][:, :], op=ALU.mult), svv))
            bank_free = [(kvts[c], "svv") for c in range(NCH)]
            k_consumed = kvts[-1]
            op("vector", lambda e_, e=e: e_.tensor_tensor_scan(
                out=kvbuf3[:, 0:T], data0=kvbuf3[:, 0:T], data1=kvbuf3[:, 0:T],
                initial=tC[:, e:e + 1], op0=ALU.add, op1=ALU.bypass))
            q_consumed = op("vector", lambda e_: e_.scalar_tensor_tensor(
                out=kvbuf3[:, 0:T], in0=qbuf[:, 0:T], scalar=tS[:, 6:7],
                in1=kvbuf3[:, 0:T], op0=ALU.mult, op1=ALU.mult), svv)
            op("vector", lambda e_: e_.tensor_copy(vh_hi[:, 0:T], kvbuf3[:, 0:T]))
            spl = op("vector", lambda e_: e_.scalar_tensor_tensor(
                out=vh_lo[:, 0:T], in0=kvbuf3[:, 0:T], scalar=0.0,
                in1=vh_hi[:, 0:T], op0=ALU.add, op1=ALU.subtract), svv)
            wait("scalar", svv, spl)
            vw1 = stVW.dma(vhat_d[0, e, :, :], vh_hi[:, 0:T])
            vw2 = stVW.dma(vhat_d[1, e, :, :], vh_lo[:, 0:T])
            vhat_free = (vw1, vw2)
            quant_step((2 * len(q_units) // 3 + NE - 1) // NE)

        ph3_mm = smm.c

        phase_barrier()
        wo_in = A32[:, 0:2048]
        stage4 = A32[:, 2048:4096]
        x5sl = [A32[:, 4096:6144]]
        t5 = A32[:, 6144:6144 + QW]
        c5 = A32[:, 7168:7168 + QW]
        d5 = A32[:, 8192:8192 + QW]
        r5 = A32[:, 9216:9216 + QW]
        NSL = D // QW

        vs_hi_sl = [A16[:, 0:NE * 256], A16[:, 2 * NE * 256:3 * NE * 256]]
        vs_lo_sl = [A16[:, NE * 256:2 * NE * 256],
                    A16[:, 3 * NE * 256:4 * NE * 256]]
        NTTG = NT // 2
        NFCW = D // 512

        prep_units = [(tt, s) for tt in range(NT) for s in range(NSL)]
        pstate = {"i": 0, "x_free": 0, "x_load": None, "af_free": 0,
                  "tb_free": [0, 0]}

        def prep_step(n):
            for _ in range(n):
                i = pstate["i"]
                if i >= len(prep_units):
                    return
                tt, s = prep_units[i]
                x5 = x5sl[0]
                if s == 0:
                    wait("sync", svv, pstate["x_free"])
                    pstate["x_load"] = stX.dma(
                        x5[:, 0:D], xm_d[tt * 128:(tt + 1) * 128, 0:D])
                wait_t("vector", pstate["x_load"])
                op("vector", lambda e, s=s, tt=tt, x5=x5: e.tensor_scalar(
                    out=t5, in0=x5[:, s * QW:(s + 1) * QW],
                    scalar1=tM[:, tt:tt + 1], scalar2=tRR[:, tt:tt + 1],
                    op0=ALU.subtract, op1=ALU.mult))
                op("vector", lambda e: e.tensor_copy(AI[:, 0:QW], t5))
                op("vector", lambda e: e.tensor_copy(r5, AI[:, 0:QW]))
                wait("vector", ssc, pstate["af_free"])
                op("vector", lambda e: e.scalar_tensor_tensor(
                    out=c5, in0=t5, scalar=-0.5, in1=r5,
                    op0=ALU.add, op1=ALU.is_gt))
                op("vector", lambda e: e.scalar_tensor_tensor(
                    out=d5, in0=t5, scalar=0.5, in1=r5,
                    op0=ALU.add, op1=ALU.is_lt))
                op("vector", lambda e: e.tensor_tensor(out=t5, in0=r5,
                                                       in1=c5, op=ALU.add))
                op("vector", lambda e: e.tensor_tensor(out=r5, in0=t5, in1=d5,
                                                       op=ALU.subtract))
                cl = op("vector", lambda e: e.tensor_scalar(
                    out=c5, in0=r5, scalar1=127.0, scalar2=-128.0,
                    op0=ALU.min, op1=ALU.max), svv)
                if s == NSL - 1:
                    pstate["x_free"] = cl
                ngr = QW // 512
                for g in range(ngr):
                    tb = (g + s * ngr) % 2
                    wait("tensor", ssc, pstate["tb_free"][tb])
                    wait("tensor", svv, cl)
                    for j in range(4):
                        col = g * 512 + j * 128
                        lam = (lambda e, col=col, tb=tb, j=j:
                               e.transpose(pb[4 + tb][:, j * 128:(j + 1) * 128],
                                           c5[:, col:col + 128], id32[:, :]))
                        if j == 3:
                            mt2 = topi(lam)
                        else:
                            op("tensor", lam)
                    wait("scalar", smm, mt2)
                    dt0 = s * (QW // 128) + g * 4
                    dst = bass.AP(S1, dt0 * T + tt * 128,
                                  [[S1W, 128], [T, 4], [1, 128]])
                    ev = op("scalar", lambda e, tb=tb, dst=dst:
                            e.copy(out=dst, in_=pb[4 + tb][:, 0:512]), ssc)
                    pstate["tb_free"][tb] = ev
                    pstate["af_free"] = ev
                pstate["i"] += 1

        fc_pairs = [tuple(range(p, min(p + 2, NFCW)))
                    for p in range(0, NFCW, 2)]
        bank_free4 = [(0, "ssc")] * 8
        stage_free = [None] * 4
        wo_sweep_done = 0
        n_prep_per_iter = max(1, (len(prep_units) + len(fc_pairs) * NTTG - 1)
                              // (len(fc_pairs) * NTTG))
        for pi, fcl in enumerate(fc_pairs):
            spl_done = 0
            for c2 in range(len(fcl)):
                fc = fcl[c2]
                for hhalf in range(NE * 512 // 2048):
                    wait("sync", svv, spl_done)
                    wait("sync", smm, wo_sweep_done)
                    lt = stWO.dma(wo_in[:, 0:2048],
                                  wo_d[fc, :, hhalf * 2048:(hhalf + 1) * 2048])
                    wait_t("vector", lt)
                    wait("vector", smm, wo_sweep_done)
                    base = c2 * 2 * NE * 512 + hhalf * 2048
                    op("vector", lambda e, base=base:
                       e.tensor_copy(S2[:, base:base + 2048], wo_in[:, 0:2048]))
                    spl_done = op("vector", lambda e, base=base:
                                  e.scalar_tensor_tensor(
                                      out=S2[:, NE * 512 + base:
                                             NE * 512 + base + 2048],
                                      in0=wo_in[:, 0:2048], scalar=0.0,
                                      in1=S2[:, base:base + 2048],
                                      op0=ALU.add, op1=ALU.subtract), svv)
            pair_split_done = spl_done
            for ttg in range(NTTG):
                gi = pi * NTTG + ttg
                par2 = gi % 2
                vs_hi = vs_hi_sl[par2]
                vs_lo = vs_lo_sl[par2]
                if gi == 0:
                    wait("sync", smm, pstate.get("vs_free_%d" % par2, 0))
                    for et in range(NE):
                        stVR.dma(vs_hi[:, et * 256:(et + 1) * 256],
                                 vhat_d[0, et, :, ttg * 256:(ttg + 1) * 256])
                        vs_tok = stVR.dma(
                            vs_lo[:, et * 256:(et + 1) * 256],
                            vhat_d[1, et, :, ttg * 256:(ttg + 1) * 256])
                    pstate["vs_toks_%d" % par2] = stVR.all_toks()
                ngi = gi + 1
                if ngi < len(fc_pairs) * NTTG:
                    npar = ngi % 2
                    nttg = ngi % NTTG
                    wait("sync", smm, pstate.get("vs_free_%d" % npar, 0))
                    for et in range(NE):
                        stVR.dma(vs_hi_sl[npar][:, et * 256:(et + 1) * 256],
                                 vhat_d[0, et, :, nttg * 256:(nttg + 1) * 256])
                        stVR.dma(vs_lo_sl[npar][:, et * 256:(et + 1) * 256],
                                 vhat_d[1, et, :, nttg * 256:(nttg + 1) * 256])
                    pstate["vs_toks_%d" % npar] = stVR.all_toks()
                for tok_ in pstate["vs_toks_%d" % par2]:
                    wait_t("tensor", tok_)
                wait("tensor", svv, pair_split_done)
                for tl_ in range(2):
                    for c2_ in range(len(fcl)):
                        bf_val, bf_sem = bank_free4[tl_ * 2 + c2_]
                        if bf_val:
                            wait("tensor", CSEM[bf_sem], bf_val)
                for et in range(NE):
                    for tl in range(2):
                        sb_h = et * 256 + tl * 128
                        for c2 in range(len(fcl)):
                            bkk = tl * 2 + c2
                            whi_b = c2 * 2 * NE * 512 + et * 512
                            wlo_b = whi_b + NE * 512
                            op("tensor", lambda e, sb_h=sb_h, bkk=bkk, wb=whi_b,
                               vh=vs_hi, s=(et == 0):
                               e.matmul(pb[bkk][:, :], vh[:, sb_h:sb_h + 128],
                                        S2[:, wb:wb + 512], start=s, stop=False))
                            op("tensor", lambda e, sb_h=sb_h, bkk=bkk, wb=wlo_b,
                               vh=vs_hi:
                               e.matmul(pb[bkk][:, :], vh[:, sb_h:sb_h + 128],
                                        S2[:, wb:wb + 512], start=False,
                                        stop=False))
                            lam = (lambda e, sb_h=sb_h, bkk=bkk, wb=whi_b,
                                   vl=vs_lo, s=(et == NE - 1):
                                   e.matmul(pb[bkk][:, :],
                                            vl[:, sb_h:sb_h + 128],
                                            S2[:, wb:wb + 512], start=False,
                                            stop=s))
                            if et == NE - 1 and tl == 1 and c2 == len(fcl) - 1:
                                mt = topi(lam)
                            else:
                                op("tensor", lam)
                for tl in range(2):
                    for c2 in range(len(fcl)):
                        fc = fcl[c2]
                        bkk = tl * 2 + c2
                        si = bkk
                        wait("scalar", smm, mt)
                        wait_t("scalar", stage_free[si])
                        ev = op("scalar", lambda e, bkk=bkk, si=si:
                                e.copy(out=stage4[:, si * 512:(si + 1) * 512],
                                       in_=pb[bkk][:, :]), ssc)
                        bank_free4[bkk] = (ev, "ssc")
                        wait("scalar", ssc, ev)
                        r0 = ttg * 256 + tl * 128
                        stage_free[si] = stOW.dma(
                            out_d[r0:r0 + 128, fc * 512:(fc + 1) * 512],
                            stage4[:, si * 512:(si + 1) * 512])
                pstate["vs_free_%d" % par2] = smm.c
                prep_step(n_prep_per_iter)
            wo_sweep_done = smm.c
        ph4_mm = smm.c
        prep_step(len(prep_units))
        prep_done_sc = ssc.c
        prep_done_vv = svv.c

        phase_barrier()
        gbuf = [A32[:, 0:512], A32[:, 512:1024]]
        hbuf = [A32[:, 1024:1536], A32[:, 1536:2048]]
        SCGb = A32[:, 2048:2048 + T]
        SCUb = A32[:, 2048 + T:2048 + 2 * T]
        HMb = A32[:, 2048 + 2 * T:2048 + 3 * T]
        WSL = NDT * 512
        wg_sl = [S2[:, 0:WSL], S2[:, WSL:2 * WSL]]
        wu_sl = [S2[:, 2 * WSL:3 * WSL], S2[:, 3 * WSL:4 * WSL]]
        assert 4 * WSL <= S2W and 2048 + 3 * T <= 8192

        op("vector", lambda e: e.memset(gbuf[0][:, 0:128], 1.0), svv)
        op("vector", lambda e: e.memset(HMb[:, 0:T], 0.0), svv)
        bc_bank_cp = [0, 0]
        bc_bank_tr = [0, 0]
        for src, dst in ((tSCG, SCGb), (tSCU, SCUb)):
            for tt in range(NT):
                bk = tt % 2
                pcol = 128 + bk * 128
                if bc_bank_tr[bk]:
                    wait("vector", smm, bc_bank_tr[bk])
                bt = op("vector", lambda e, src=src, tt=tt, pcol=pcol:
                        e.tensor_scalar(out=gbuf[0][:, pcol:pcol + 128],
                                        in0=gbuf[0][:, 0:128],
                                        scalar1=src[:, tt:tt + 1],
                                        scalar2=None, op0=ALU.mult), svv)
                wait("tensor", svv, bt)
                wait("tensor", ssc, bc_bank_cp[bk])
                mt = topi(lambda e, bk=bk, pcol=pcol:
                          e.transpose(pb[bk][:, 0:128],
                                      gbuf[0][:, pcol:pcol + 128], id32[:, :]))
                bc_bank_tr[bk] = mt
                wait("scalar", smm, mt)
                bc_bank_cp[bk] = op("scalar", lambda e, bk=bk, dst=dst, tt=tt:
                                    e.copy(out=dst[:, tt * 128:(tt + 1) * 128],
                                           in_=pb[bk][:, 0:128]), ssc)
        bc_done = ssc.c

        def w8_ap(w8d, fc):
            return bass.AP(w8d, fc * NDT * 128 * 512,
                           [[512, 128], [128 * 512, NDT], [1, 512]])

        wgu_free = [0, 0]
        gu_load_t = {}

        def load_gu(fc):
            if fc >= NFC:
                return
            sl = fc % 2
            wait("sync", smm, wgu_free[sl])
            t1 = stQ8R.dma(wg_sl[sl], w8_ap(wg8_d, fc))
            t2 = stQ8R.dma(wu_sl[sl], w8_ap(wu8_d, fc))
            gu_load_t[fc] = (t1, t2)

        load_gu(0)
        load_gu(1)
        pair_free = [0, 0, 0, 0]
        g_free = [0, 0]
        h_free = [None, None]
        first_sig = False
        itg = 0
        for fc in range(NFC):
            sl = fc % 2
            wait_t("tensor", gu_load_t[fc][0])
            wait_t("tensor", gu_load_t[fc][1])
            if fc == 0:
                wait("tensor", ssc, prep_done_sc)
                wait("tensor", ssc, bc_done)
            for fq in range(4):
                ffd = fc * 4 + fq
                for w in range(NCH):
                    pr = itg % 4
                    wait("tensor", svv, pair_free[pr])
                    for dt in range(NDT):
                        lcol = dt * 512 + fq * 128
                        rcol = dt * T + w * 512
                        op("tensor", lambda e, lcol=lcol, rcol=rcol, pr=pr,
                           sl=sl, s=(dt == 0), p=(dt == NDT - 1):
                           e.matmul(pb[pr * 2][:, :],
                                    wg_sl[sl][:, lcol:lcol + 128],
                                    S1[:, rcol:rcol + 512], start=s, stop=p))
                        lam = (lambda e, lcol=lcol, rcol=rcol, pr=pr, sl=sl,
                               s=(dt == 0), p=(dt == NDT - 1):
                               e.matmul(pb[pr * 2 + 1][:, :],
                                        wu_sl[sl][:, lcol:lcol + 128],
                                        S1[:, rcol:rcol + 512],
                                        start=s, stop=p))
                        if dt == NDT - 1:
                            mt = topi(lam)
                        else:
                            op("tensor", lam)
                    gsl = itg % 2
                    hsl = itg % 2
                    wait("vector", smm, mt)
                    wait_t("vector", h_free[hsl])
                    pg = op("vector", lambda e, pr=pr, hsl=hsl, w=w:
                            e.tensor_tensor(out=hbuf[hsl],
                                            in0=pb[pr * 2][:, :],
                                            in1=SCGb[:, w * 512:(w + 1) * 512],
                                            op=ALU.mult), svv)
                    if not first_sig:
                        wait("scalar", svv, prep_done_vv)
                        first_sig = True
                    wait("scalar", svv, pg)
                    wait("scalar", svv, g_free[gsl])
                    sg = op("scalar", lambda e, gsl=gsl, hsl=hsl:
                            e.activation(out=gbuf[gsl], in_=hbuf[hsl],
                                         func=ACT.Sigmoid), ssc)
                    wait("vector", ssc, sg)
                    op("vector", lambda e, pr=pr, hsl=hsl, w=w:
                       e.tensor_tensor(out=hbuf[hsl], in0=pb[pr * 2 + 1][:, :],
                                       in1=SCUb[:, w * 512:(w + 1) * 512],
                                       op=ALU.mult))
                    pu = op("vector", lambda e, hsl=hsl, gsl=gsl:
                            e.tensor_tensor(out=hbuf[hsl], in0=hbuf[hsl],
                                            in1=gbuf[gsl], op=ALU.mult), svv)
                    pair_free[pr] = pu
                    g_free[gsl] = pu
                    op("vector", lambda e, hsl=hsl, w=w:
                       e.tensor_tensor(out=HMb[:, w * 512:(w + 1) * 512],
                                       in0=HMb[:, w * 512:(w + 1) * 512],
                                       in1=hbuf[hsl], op=ALU.max))
                    op("vector", lambda e, hsl=hsl, w=w:
                       e.scalar_tensor_tensor(
                           out=HMb[:, w * 512:(w + 1) * 512],
                           in0=hbuf[hsl], scalar=-1.0,
                           in1=HMb[:, w * 512:(w + 1) * 512],
                           op0=ALU.mult, op1=ALU.max), svv)
                    wait("scalar", svv, pu)
                    h_free[hsl] = stHW.dma(h_d[ffd, :, w * 512:(w + 1) * 512],
                                           hbuf[hsl])
                    itg += 1
                    if itg % 2 == 0:
                        quant_step(1)
            wgu_free[sl] = smm.c
            load_gu(fc + 2)
        quant_step(len(q_units))
        ph5a_mm = smm.c
        hm_last = svv.c

        for pr in range(4):
            wait("tensor", svv, pair_free[pr])
        wait("tensor", svv, hm_last)
        hm_red = [0, 0]
        for tt in range(NT):
            bk = tt % 2
            wait("tensor", svv, hm_red[bk])
            mt = topi(lambda e, bk=bk, tt=tt:
                      e.transpose(pb[bk][:, 0:128],
                                  HMb[:, tt * 128:(tt + 1) * 128], id32[:, :]))
            wait("vector", smm, mt)
            hm_red[bk] = op("vector", lambda e, bk=bk, tt=tt:
                            e.tensor_reduce(out=tHM[:, tt:tt + 1],
                                            in_=pb[bk][:, 0:128], axis=AX,
                                            op=ALU.max), svv)
        hm_tr_done = smm.c


        op("vector", lambda e: e.tensor_scalar(out=tHM[:, 0:NT], in0=tHM[:, 0:NT],
                                               scalar1=EPS, scalar2=None,
                                               op0=ALU.max))
        op("vector", lambda e: e.reciprocal(out=tSCD[:, 0:NT], in_=tHM[:, 0:NT]))
        op("vector", lambda e: e.tensor_scalar(out=tSCD[:, 0:NT],
                                               in0=tSCD[:, 0:NT],
                                               scalar1=consts[:, 2:3],
                                               scalar2=None, op0=ALU.mult))
        op("vector", lambda e: e.tensor_copy(tRR[:, 0:NT], tSCD[:, 0:NT]))
        hth = op("vector", lambda e: e.tensor_scalar(out=tSCD[:, 0:NT],
                                                     in0=tHM[:, 0:NT],
                                                     scalar1=tS[:, 10:11],
                                                     scalar2=None,
                                                     op0=ALU.mult), svv)

        h5sl = [A32[:, 0:1024], A32[:, 1024:2048], A32[:, 2048:3072]]
        t5b = A32[:, 3072:4096]
        c5b = A32[:, 4096:5120]
        r5b = A32[:, 5120:6144]
        d5b = A32[:, 6144:7168]
        xc5 = A32[:, 7168:9216]
        xtemp = A32[:, 7168:7168 + T]
        RRBw = [A32[:, 9216:9472], A32[:, 9472:9728]]
        AHW = NFFD * 256
        buf5 = [S2[:, 0:AHW], S1[:, 0:AHW]]
        wd_sl = [S2[:, AHW:AHW + 4 * D], S2[:, AHW + 4 * D:AHW + 8 * D],
                 S1[:, AHW:AHW + 4 * D]]
        assert AHW + 8 * D <= S2W and AHW + 4 * D <= S1W
        NOC = D // 512
        NST = NT // 2
        NSLAB = NFFD // 4

        wait("vector", ssc, ssc.c)
        op("vector", lambda e: e.memset(gbuf[0][:, 0:128], 1.0), svv)
        rrb_tr = [0, 0]
        rrb_cp = [0, 0]
        for tt in range(NT):
            bk = tt % 2
            pcol = 128 + bk * 128
            if rrb_tr[bk]:
                wait("vector", smm, rrb_tr[bk])
            bt = op("vector", lambda e, tt=tt, pcol=pcol:
                    e.tensor_scalar(out=gbuf[0][:, pcol:pcol + 128],
                                    in0=gbuf[0][:, 0:128],
                                    scalar1=tRR[:, tt:tt + 1],
                                    scalar2=None, op0=ALU.mult), svv)
            wait("tensor", svv, bt)
            wait("tensor", ssc, rrb_cp[bk])
            if tt < 2:
                wait("tensor", svv, hm_red[bk])
            mt = topi(lambda e, bk=bk, pcol=pcol:
                      e.transpose(pb[bk][:, 0:128],
                                  gbuf[0][:, pcol:pcol + 128], id32[:, :]))
            rrb_tr[bk] = mt
            wait("scalar", smm, mt)
            rrb_cp[bk] = op("scalar", lambda e, bk=bk, tt=tt:
                            e.copy(out=xtemp[:, tt * 128:(tt + 1) * 128],
                                   in_=pb[bk][:, 0:128]), ssc)
        rrb_done_sc = ssc.c

        wait("sync", svv, svv.c)
        wait("sync", ssc, ssc.c)
        wait("sync", smm, smm.c)
        wait_all("sync", stQ8W)
        wait("scalar", svv, svv.c)
        wait("scalar", smm, smm.c)
        wait_all("scalar", stHW)
        wait_all("scalar", stQ8W)

        rrb_wr = stX.dma(rrb_d[:, 0:T], xtemp[:, 0:T])
        rrbw_tok = [None, None]
        rrbw_war = [0, 0]

        def load_rrbw(st):
            if st >= NST:
                return
            sl = st % 2
            wait("sync", svv, rrbw_war[sl])
            wait_t("sync", rrb_wr)
            rrbw_tok[sl] = stX.dma(RRBw[sl],
                                   rrb_d[:, st * 256:st * 256 + 256])

        bank5_free = [0] * 8
        wd_free = [ph5a_mm] * 3
        wd_load_t = {}
        h5_free = [0, 0, 0]
        evtok = {}
        bst = {"k": 0, "c5_war": 0}
        xc_free = {"t": 0}
        st_free = [None]
        first_ev = {"done": False}
        wd_items = [(sti, gi) for sti in range(NST) for gi in range(NSLAB)]

        def wd_ap(grp):
            return bass.AP(wd8_d, grp * 4 * 128 * D,
                           [[D, 128], [128 * D, 4], [1, D]])

        def load_wd_n(n):
            if n >= len(wd_items):
                return
            sl = n % 3
            wait("sync", smm, wd_free[sl])
            wd_load_t[n] = stQ8R.dma(wd_sl[sl], wd_ap(wd_items[n][1]))

        def build_slab(st, s):
            if st >= NST:
                return
            hs = bst["k"] % 3
            bst["k"] += 1
            hslot = h5sl[hs]
            wait("scalar", svv, h5_free[hs])
            rb = stHR.dma(hslot, bass.AP(h_d, (4 * s) * 128 * T + st * 256,
                                         [[T, 128], [128 * T, 4], [1, 256]]))
            wait_t("vector", rb)
            wait_t("vector", rrbw_tok[st % 2])
            t1 = 0
            for q in range(4):
                t1 = op("vector", lambda e, q=q, hslot=hslot, st=st:
                        e.tensor_tensor(out=t5b[:, q * 256:(q + 1) * 256],
                                        in0=hslot[:, q * 256:(q + 1) * 256],
                                        in1=RRBw[st % 2][:, 0:256],
                                        op=ALU.mult), svv)
            h5_free[hs] = t1
            rrbw_war[st % 2] = t1
            op("vector", lambda e: e.tensor_copy(AI[:, 0:1024], t5b))
            op("vector", lambda e: e.tensor_copy(r5b, AI[:, 0:1024]))
            wait("vector", ssc, bst["c5_war"])
            op("vector", lambda e: e.scalar_tensor_tensor(
                out=c5b, in0=t5b, scalar=-0.5, in1=r5b,
                op0=ALU.add, op1=ALU.is_gt))
            op("vector", lambda e: e.tensor_tensor(out=c5b, in0=r5b, in1=c5b,
                                                   op=ALU.add))
            op("vector", lambda e: e.scalar_tensor_tensor(
                out=d5b, in0=t5b, scalar=0.5, in1=r5b,
                op0=ALU.add, op1=ALU.is_lt))
            t7 = op("vector", lambda e: e.tensor_tensor(out=c5b, in0=c5b,
                                                        in1=d5b,
                                                        op=ALU.subtract), svv)
            wait("scalar", svv, t7)
            t9 = op("scalar", lambda e, st=st, s=s:
                    e.copy(out=buf5[st % 2][:, s * 1024:(s + 1) * 1024],
                           in_=c5b), ssc)
            bst["c5_war"] = t9
            evtok[(st, s)] = t9

        def sweep_grp(st, g):
            n = st * NSLAB + g
            sl = n % 3
            wait("tensor", ssc, evtok[(st, g)])
            wait_t("tensor", wd_load_t[n])
            if g == 0:
                if st == 0:
                    wait("tensor", ssc, rrb_done_sc)
                for bkk in range(2 * NOC):
                    wait("tensor", svv, bank5_free[bkk])
            for sub in range(4):
                ffd = g * 4 + sub
                for tl in range(2):
                    lcol = ffd * 256 + tl * 128
                    for oc in range(NOC):
                        lam = (lambda e, lcol=lcol, tl=tl, oc=oc, sl=sl, st=st,
                               sub=sub, s=(ffd == 0), p=(ffd == NFFD - 1):
                               e.matmul(pb[tl * NOC + oc][:, :],
                                        buf5[st % 2][:, lcol:lcol + 128],
                                        wd_sl[sl][:, sub * D + oc * 512:
                                                  sub * D + oc * 512 + 512],
                                        start=s, stop=p))
                        if sub == 3 and tl == 1 and oc == NOC - 1:
                            wd_free[sl] = topi(lam)
                        else:
                            op("tensor", lam)

        def evac(st, sweep_mt):
            for tl in range(2):
                tt = st * 2 + tl
                wait("sync", svv, xc_free["t"])
                if not first_ev["done"]:
                    wait("sync", smm, hm_tr_done)
                    wait_t("sync", rrb_wr)
                    first_ev["done"] = True
                wait_t("sync", st_free[0])
                xt = stX.dma(xc5[:, 0:D], xm_d[tt * 128:(tt + 1) * 128, 0:D])
                wait("vector", smm, sweep_mt)
                wait_t("vector", xt)
                wait("vector", svv, hth)
                evt = 0
                for oc in range(NOC):
                    bkk = tl * NOC + oc
                    evt = op("vector", lambda e, bkk=bkk, oc=oc, tt=tt:
                             e.scalar_tensor_tensor(
                                 out=xc5[:, oc * 512:(oc + 1) * 512],
                                 in0=pb[bkk][:, :],
                                 scalar=tSCD[:, tt:tt + 1],
                                 in1=xc5[:, oc * 512:(oc + 1) * 512],
                                 op0=ALU.mult, op1=ALU.add), svv)
                    bank5_free[bkk] = evt
                xc_free["t"] = evt
                wait("gpsimd", svv, evt)
                st_free[0] = stOA.dma(out_d[tt * 128:(tt + 1) * 128, 0:D],
                                      xc5[:, 0:D], accum_op=ALU.add)

        SERIAL5B = False
        if SERIAL5B:
            for n0 in range(3):
                load_wd_n(n0)
            for st in range(NST):
                load_rrbw(st)
                for s in range(NSLAB):
                    build_slab(st, s)
                for k in range(NSLAB):
                    sweep_grp(st, k)
                    load_wd_n(st * NSLAB + k + 3)
                evac(st, smm.c)
        else:
            for n0 in range(3):
                load_wd_n(n0)
            load_rrbw(0)
            wait("vector", smm, hm_tr_done)
            for s in range(NSLAB):
                build_slab(0, s)
            for st in range(NST):
                load_rrbw(st + 1)
                for k in range(NSLAB):
                    build_slab(st + 1, k)
                    sweep_grp(st, k)
                    load_wd_n(st * NSLAB + k + 3)
                evac(st, smm.c)

        wait_all("gpsimd", stOA)
        wait_all("gpsimd", stOW)
        wait_all("sync", stOA)

        with nc.Block() as block:
            @block.sync
            def _(e):
                for f in S["sync"]:
                    f(e)

            @block.tensor
            def _(e):
                for f in S["tensor"]:
                    f(e)

            @block.vector
            def _(e):
                for f in S["vector"]:
                    f(e)

            @block.scalar
            def _(e):
                for f in S["scalar"]:
                    f(e)

            @block.gpsimd
            def _(e):
                for f in S["gpsimd"]:
                    f(e)

    return nc


_PROG = {}


def _get_prog(cfg_key):
    if cfg_key not in _PROG:
        _PROG[cfg_key] = build_program(dict(zip(("D", "T", "FF"), cfg_key)))
    return _PROG[cfg_key]


def make_host_inputs(cfg, Wq, Wk, Wv, Wo, Wg, Wu, Wdn):
    D, T, FF = cfg["D"], cfg["T"], cfg["FF"]
    NE = D // 128
    NDT = D // 128
    NFFD = FF // 128

    def att_img(W):
        a = np.ascontiguousarray(W.T).reshape(NDT, 128, NE, 128)
        return np.ascontiguousarray(a.transpose(2, 1, 0, 3).reshape(NE, 128, D))

    def mov_img(W, nblk):
        wt = np.ascontiguousarray(W.T)
        return np.ascontiguousarray(wt.reshape(nblk, 128, wt.shape[1]))

    d = {}
    d["wq"] = att_img(Wq)
    d["wk"] = att_img(Wk)
    d["wv"] = att_img(Wv)
    NFCW = D // 512
    wot = np.ascontiguousarray(Wo.T).reshape(NE, 128, NFCW, 512)
    d["wo"] = np.ascontiguousarray(
        wot.transpose(2, 1, 0, 3).reshape(NFCW, 128, NE * 512))
    d["wg"] = mov_img(Wg, NDT)
    d["wu"] = mov_img(Wu, NDT)
    d["wd"] = mov_img(Wdn, NFFD)
    d["id32"] = np.eye(128, dtype=np.float32)
    d["ones16"] = np.ones((128, 2), dtype=ml_dtypes.bfloat16)
    d["onesr16"] = np.ones((1, 128), dtype=ml_dtypes.bfloat16)
    consts = np.zeros((128, 8), dtype=np.float32)
    consts[:, 0] = EPS
    consts[:, 1] = 1.0 / D
    consts[:, 2] = 127.0
    consts[:, 3] = float(D)
    consts[:, 4] = 1.0
    d["consts"] = consts
    return d


def kernel(L0_attn_x, L0_attn_Wq, L0_attn_Wk, L0_attn_Wv, L0_attn_Wo,
           L0_ffn_w_gate, L0_ffn_w_up, L0_ffn_w_down):
    cfg = full_cfg()
    D, T, FF = cfg["D"], cfg["T"], cfg["FF"]
    x = np.ascontiguousarray(np.asarray(L0_attn_x, dtype=np.float32))
    B_, S_, D_ = x.shape
    assert D_ == D and S_ == 2 * T

    shared = make_host_inputs(
        cfg,
        np.asarray(L0_attn_Wq, np.float32), np.asarray(L0_attn_Wk, np.float32),
        np.asarray(L0_attn_Wv, np.float32), np.asarray(L0_attn_Wo, np.float32),
        np.asarray(L0_ffn_w_gate, np.float32),
        np.asarray(L0_ffn_w_up, np.float32),
        np.asarray(L0_ffn_w_down, np.float32))

    zeros_pref = np.zeros((T, D), dtype=np.float32)
    in_maps = []
    for c in range(8):
        b, h = c // 2, c % 2
        m = dict(shared)
        m["xm"] = np.ascontiguousarray(x[b, h * T:(h + 1) * T])
        m["xp"] = np.ascontiguousarray(x[b, 0:T]) if h == 1 else zeros_pref
        in_maps.append(m)

    nc_prog = _get_prog((D, T, FF))
    res = run_bass_kernel_spmd(nc_prog, in_maps, core_ids=list(range(8)))
    out = np.empty((B_, S_, D), dtype=np.float32)
    for c in range(8):
        b, h = c // 2, c % 2
        out[b, h * T:(h + 1) * T] = np.asarray(res.results[c]["out"])
    return out


# version 18
# speedup vs baseline: 1.0447x; 1.0104x over previous
import numpy as np
import ml_dtypes
from contextlib import ExitStack

import concourse.bass as bass
import concourse.mybir as mybir
from concourse.bass_utils import run_bass_kernel_spmd

F32 = mybir.dt.float32
BF16 = mybir.dt.bfloat16
FP8 = mybir.dt.float8e4
I32 = mybir.dt.int32
AX = mybir.AxisListType.X
ALU = mybir.AluOpType
ACT = mybir.ActivationFunctionType
EPS = 1e-5

ENGS = ("sync", "tensor", "vector", "scalar", "gpsimd")


def full_cfg():
    return dict(D=2048, T=2048, FF=8192)


class Sem:
    __slots__ = ("h", "c")

    def __init__(self, stack, nc, name):
        self.h = stack.enter_context(nc.semaphore(name))
        self.c = 0

    def bump(self, n):
        self.c += n
        return self.c


def build_program(cfg):
    D, T, FF = cfg["D"], cfg["T"], cfg["FF"]
    NT = T // 128
    NE = D // 128
    NDT = D // 128
    NFFD = FF // 128
    NFC = FF // 512
    NCH = T // 512
    QW = 1024 if (D % 1024 == 0 and FF % 1024 == 0) else 512
    NWINV = 1.0 / (D * FF)
    SEQ = 2 * T
    assert NT % 2 == 0 and NCH >= 1 and D % QW == 0 and FF % QW == 0

    nc = bass.Bass("TRN2", target_bir_lowering=False)

    xm_d = nc.dram_tensor("xm", [T, D], F32, kind="ExternalInput")
    xp_d = nc.dram_tensor("xp", [T, D], F32, kind="ExternalInput")
    wq_d = nc.dram_tensor("wq", [NE, 128, D], F32, kind="ExternalInput")
    wk_d = nc.dram_tensor("wk", [NE, 128, D], F32, kind="ExternalInput")
    wv_d = nc.dram_tensor("wv", [NE, 128, D], F32, kind="ExternalInput")
    wo_d = nc.dram_tensor("wo", [D // 512, 128, NE * 512], F32, kind="ExternalInput")
    wg_d = nc.dram_tensor("wg", [NDT, 128, FF], F32, kind="ExternalInput")
    wu_d = nc.dram_tensor("wu", [NDT, 128, FF], F32, kind="ExternalInput")
    wd_d = nc.dram_tensor("wd", [NFFD, 128, D], F32, kind="ExternalInput")
    id32_d = nc.dram_tensor("id32", [128, 128], F32, kind="ExternalInput")
    ones_d = nc.dram_tensor("ones16", [128, 2], BF16, kind="ExternalInput")
    onesr_d = nc.dram_tensor("onesr16", [1, 128], BF16, kind="ExternalInput")
    consts_d = nc.dram_tensor("consts", [128, 8], F32, kind="ExternalInput")
    out_d = nc.dram_tensor("out", [T, D], F32, kind="ExternalOutput")

    vhat_d = nc.dram_tensor("vhat_s", [2, NE, 128, T], BF16)
    h_d = nc.dram_tensor("h_s", [NFFD, 128, T], F32)
    rrb_d = nc.dram_tensor("rrb_s", [128, T], F32)
    wg8_d = nc.dram_tensor("wg8_s", [NFC, NDT, 128, 512], BF16)
    wu8_d = nc.dram_tensor("wu8_s", [NFC, NDT, 128, 512], BF16)
    wd8_d = nc.dram_tensor("wd8_s", [NFFD, 128, D], BF16)

    with ExitStack() as stack:
        sb = lambda nm, shp, dt=F32: stack.enter_context(nc.sbuf_tensor(nm, shp, dt))

        S1W = max(NDT * T, NE * D, NFFD * 256 + 4 * D)
        S2W = max(NDT * T, NE * D, NFFD * 256 + 2 * 4 * D, 4 * NDT * 512)
        A32 = sb("A32", [128, 10240])
        S1 = sb("S1", [128, S1W], BF16)
        S2 = sb("S2", [128, S2W], BF16)
        A16 = sb("A16", [128, 16384], BF16)
        AI = sb("AI", [128, max(QW, 1024)], I32)
        id32 = sb("id32s", [128, 128])
        ones = sb("ones16s", [128, 2], BF16)
        onesr = sb("onesr16s", [1, 128], BF16)
        consts = sb("constss", [128, 8])
        nc.const_aps.register(F32, EPS, consts[:, 0:1])
        nc.const_aps.register(F32, 1.0 / D, consts[:, 1:2])
        nc.const_aps.register(F32, 1.0, consts[:, 4:5])

        tC = sb("tC", [128, NE])
        tM = sb("tM", [128, NT])
        tNM = sb("tNM", [128, NT])
        tR = sb("tR", [128, NT])
        tRM = sb("tRM", [128, NT])
        tRR = sb("tRR", [128, NT])
        tSCG = sb("tSCG", [128, NT])
        tSCU = sb("tSCU", [128, NT])
        tSCD = sb("tSCD", [128, NT])
        tHM = sb("tHM", [128, NT])
        tS = sb("tS", [128, 12])
        tSn = sb("tSn", [128, 4])
        tACC = sb("tACC", [128, 4])
        tT1 = sb("tT1", [128, 8])
        tT2 = sb("tT2", [128, 8])
        tSum = sb("tSum", [1, 4])
        tSplit = sb("tSplit", [128, 4], BF16)

        pb = [stack.enter_context(nc.psum_tensor(f"pb{i}", [128, 512], F32))
              for i in range(8)]

        ALL_SEMS = []

        def mksem(name):
            s = Sem(stack, nc, name)
            ALL_SEMS.append(s)
            return s

        smm = mksem("smm")
        ssc = mksem("ssc")
        svv = mksem("svv")
        sgc = mksem("sgc")
        CSEM = {"smm": smm, "ssc": ssc, "svv": svv, "sgc": sgc}

        S = {k: [] for k in ENGS}

        def em(eng, fn):
            S[eng].append(fn)

        def wait(eng, sem, val):
            if val and val > 0:
                em(eng, lambda e, s=sem.h, v=val: e.wait_ge(s, v))

        def wait_t(eng, tok):
            if tok:
                wait(eng, tok[0], tok[1])

        SELF = {"vector": svv, "scalar": ssc, "gpsimd": sgc}

        def op(eng, fn, sem=None, n=1):
            if eng in SELF:
                s = SELF[eng]
                assert sem is None or sem is s
                wait(eng, s, s.c)
                t = s.bump(1)
                em(eng, lambda e, f=fn, sh=s.h: f(e).then_inc(sh, 1))
                return t
            if sem is None:
                em(eng, fn)
                return None
            t = sem.bump(n)
            em(eng, lambda e, f=fn, s=sem.h, k=n: f(e).then_inc(s, k))
            return t

        def topi(fn):
            t = smm.bump(1)
            em("tensor", lambda e, f=fn, sh=smm.h: f(e).then_inc(sh, 1))
            return t

        class DS:

            def __init__(self, name, eng, nsem=2):
                self.sems = [mksem(f"{name}{i}") for i in range(nsem)]
                self.eng = eng
                self.i = 0

            def dma(self, out, in_, **kw):
                s = self.sems[self.i % len(self.sems)]
                self.i += 1
                wait(self.eng, s, s.c)
                t = s.bump(16)
                em(self.eng, lambda e, o=out, i_=in_, kw=dict(kw), sh=s.h:
                   e.dma_start(out=o, in_=i_, **kw).then_inc(sh, 16))
                return (s, t)

            def all_toks(self):
                return [(s, s.c) for s in self.sems]

        def wait_all(eng, stream):
            for tok in stream.all_toks():
                wait_t(eng, tok)

        stX = DS("sx", "sync", 3)
        stWA = DS("swa", "sync", 3)
        stWM = DS("swm", "sync", 3)
        stWQ = DS("swq", "sync", 3)
        stWO = DS("swo", "sync", 2)
        stVR = DS("svr", "sync", 4)
        stHR = DS("shr", "scalar", 3)
        stVW = DS("svw", "scalar", 2)
        stHW = DS("shw", "scalar", 2)
        stOW = DS("sow", "scalar", 2)
        stQ8W = DS("sq8w", "scalar", 2)
        stQ8R = DS("sq8r", "sync", 3)
        stOA = DS("soa", "gpsimd", 1)

        def phase_barrier():
            vals = [(s, s.c) for s in ALL_SEMS]
            for eng in ENGS:
                for s_, v_ in vals:
                    wait(eng, s_, v_)

        for o, i in ((id32[:, :], id32_d[:, :]), (ones[:, :], ones_d[:, :]),
                     (onesr[:, :], onesr_d[:, :]), (consts[:, :], consts_d[:, :])):
            stX.dma(o, i)
        phase_barrier()

        wimgs = [(wg_d, NDT, FF, 0), (wu_d, NDT, FF, 1), (wd_d, NFFD, D, 2)]
        mean_units = []
        for wdd, nblk, blkw, wi in wimgs:
            for bi in range(nblk):
                for c in range(blkw // QW):
                    mean_units.append((wdd[bi, :, c * QW:(c + 1) * QW], wi))
        wm_slot = [A32[:, 6144:6144 + QW], A32[:, 7168:7168 + QW],
                   A32[:, 8192:8192 + QW], A32[:, 9216:9216 + QW]]
        mu = {"i": 0, "load_t": {}, "free_t": {}}

        def mean_issue_load(idx):
            if idx >= len(mean_units):
                return
            src, _ = mean_units[idx]
            slot = idx % 4
            ft = mu["free_t"].get(slot)
            if ft:
                wait("sync", svv, ft)
            mu["load_t"][idx] = stWM.dma(wm_slot[slot], src)

        def mean_step(nunits):
            for _ in range(nunits):
                i = mu["i"]
                if i >= len(mean_units):
                    return
                if i == 0:
                    for k_ in range(4):
                        mean_issue_load(k_)
                _, wi = mean_units[i]
                slot = i % 4
                wait_t("vector", mu["load_t"][i])
                op("vector", lambda e, s=wm_slot[slot]:
                   e.tensor_reduce(out=tT1[:, 0:1], in_=s, axis=AX, op=ALU.add,
                                   apply_absolute_value=True))
                ft = op("vector", lambda e, wi=wi:
                        e.tensor_tensor(out=tACC[:, wi:wi + 1],
                                        in0=tACC[:, wi:wi + 1],
                                        in1=tT1[:, 0:1], op=ALU.add), svv)
                mu["free_t"][slot] = ft
                mean_issue_load(i + 4)
                mu["i"] += 1

        op("vector", lambda e: e.memset(tACC[:, 0:4], 0.0), svv)

        q_units = []
        for wdd, nblk, blkw, wi in wimgs:
            for bi in range(nblk):
                for c in range(blkw // QW):
                    src = wdd[bi, :, c * QW:(c + 1) * QW]
                    if wi < 2:
                        w8 = wg8_d if wi == 0 else wu8_d
                        dsts = [w8[(c * QW) // 512 + j, bi, :, :]
                                for j in range(QW // 512)]
                    else:
                        dsts = [wd8_d[bi, :, c * QW:(c + 1) * QW]]
                    q_units.append((src, wi, dsts))
        q_in_sl = [A32[:, 8192:8192 + QW], A32[:, 9216:9216 + QW]]
        qp_buf = A16[:, 14336:14336 + QW]
        qn_buf = A16[:, 15360:15360 + QW]
        qb_buf = A16[:, 12288:12288 + QW]
        qu = {"i": 0, "wq_free": None, "gu_done": None, "th_ready": 0}

        def quant_step(nunits):
            for _ in range(nunits):
                i = qu["i"]
                if i >= len(q_units):
                    return
                src, wi, dsts = q_units[i]
                q_in = q_in_sl[i % 2]
                wait("sync", svv, qu.get("b_done_%d" % (i % 2), 0))
                lt = stWQ.dma(q_in, src)
                wait_t("vector", lt)
                op("vector", lambda e, wi=wi, q_in=q_in:
                   e.tensor_scalar(out=qp_buf, in0=q_in,
                                   scalar1=tS[:, 3 + wi:4 + wi], scalar2=None,
                                   op0=ALU.is_gt))
                nt_ = op("vector", lambda e, wi=wi, q_in=q_in:
                         e.tensor_scalar(out=qn_buf, in0=q_in,
                                         scalar1=tSn[:, wi:wi + 1], scalar2=None,
                                         op0=ALU.is_lt))
                qu["b_done_%d" % (i % 2)] = nt_
                for tok_ in (qu["wq_free"] or []):
                    wait_t("vector", tok_)
                bt = op("vector", lambda e:
                        e.tensor_tensor(out=qb_buf, in0=qp_buf, in1=qn_buf,
                                        op=ALU.subtract), svv)

                wait("scalar", svv, bt)
                nj = len(dsts)
                w = QW // nj
                wts = []
                for j, dst in enumerate(dsts):
                    wts.append(stQ8W.dma(dst, qb_buf[:, j * w:(j + 1) * w]))
                qu["wq_free"] = wts
                if wi < 2:
                    qu["gu_done"] = wts[-1]
                qu["i"] += 1

        xslot = [A32[:, 0:2048], A32[:, 2048:4096]]
        xnslot = [A32[:, 4096:6144], A32[:, 6144:8192]]
        ln = {"xfree": [[0, 0], [0, 0]], "pair_tr": 0, "bankf": [0, 0]}

        def ln_phase(xd, save_stats):
            for pair in range(NT // 2):
                vts = {}
                sts = {}
                xts = {}
                for half in range(2):
                    i = pair * 2 + half
                    sl = half
                    wait("sync", svv, ln["xfree"][sl][0])
                    wait("sync", ssc, ln["xfree"][sl][1])
                    xt = stX.dma(xslot[sl][:, 0:D], xd[i * 128:(i + 1) * 128, 0:D])
                    xts[i] = xt
                    wait_t("vector", xt)
                    op("vector", lambda e, sl=sl:
                       e.reduce_sum(out=tT1[:, 0:1], in_=xslot[sl][:, 0:D], axis=AX))
                    op("vector", lambda e, i=i:
                       e.tensor_scalar(out=tM[:, i:i + 1], in0=tT1[:, 0:1],
                                       scalar1=1.0 / D, scalar2=None, op0=ALU.mult))
                    vt = op("vector", lambda e, i=i:
                            e.tensor_scalar(out=tNM[:, i:i + 1], in0=tT1[:, 0:1],
                                            scalar1=-1.0 / D, scalar2=None,
                                            op0=ALU.mult), svv)
                    vts[i] = vt
                    wait("scalar", svv, vt)
                    wait("scalar", smm, ln["pair_tr"])
                    wait_t("scalar", xt)
                    op("scalar", lambda e, sl=sl, i=i:
                       e.activation(out=xnslot[sl][:, 0:D], in_=xslot[sl][:, 0:D],
                                    func=ACT.Square, bias=tNM[:, i:i + 1],
                                    accum_out=tT1[:, 1 + sl:2 + sl]))
                    st_ = op("scalar", lambda e, sl=sl:
                             e.activation(out=tT2[:, 4 + sl:5 + sl],
                                          in_=tT1[:, 1 + sl:2 + sl],
                                          func=ACT.Sqrt, scale=1.0 / D, bias=EPS),
                             ssc)
                    sts[i] = st_
                for half in range(2):
                    i = pair * 2 + half
                    sl = half
                    wait("vector", ssc, sts[i])
                    wait("vector", smm, ln["pair_tr"])
                    op("vector", lambda e, i=i, sl=sl:
                       e.reciprocal(out=tR[:, i:i + 1], in_=tT2[:, 4 + sl:5 + sl]))
                    xnt = op("vector", lambda e, sl=sl, i=i:
                             e.tensor_scalar(out=xnslot[sl][:, 0:D],
                                             in0=xslot[sl][:, 0:D],
                                             scalar1=tM[:, i:i + 1],
                                             scalar2=tR[:, i:i + 1],
                                             op0=ALU.subtract, op1=ALU.mult), svv)
                    if save_stats:
                        op("vector", lambda e, sl=sl, i=i:
                           e.tensor_reduce(out=tRM[:, i:i + 1],
                                           in_=xnslot[sl][:, 0:D], axis=AX,
                                           op=ALU.max, apply_absolute_value=True))
                    ln["xfree"][sl] = [xnt, sts[i]]
                xn_ready = svv.c
                wait("tensor", svv, xn_ready)
                for f in range(NE):
                    bk = 4 + (f % 2)
                    wait("tensor", svv, ln["bankf"][f % 2])
                    op("tensor", lambda e, bk=bk, f=f:
                       e.transpose(pb[bk][:, 0:128],
                                   xnslot[0][:, f * 128:(f + 1) * 128],
                                   id32[:, :]))
                    mt = topi(lambda e, bk=bk, f=f:
                              e.transpose(pb[bk][:, 128:256],
                                          xnslot[1][:, f * 128:(f + 1) * 128],
                                          id32[:, :]))
                    wait("scalar", smm, mt)
                    base = f * T + pair * 256
                    ht = op("scalar", lambda e, bk=bk, base=base:
                            e.copy(out=S1[:, base:base + 256],
                                   in_=pb[bk][:, 0:256]), ssc)
                    wait("vector", ssc, ht)
                    lt_ = op("vector", lambda e, bk=bk, base=base:
                             e.scalar_tensor_tensor(out=S2[:, base:base + 256],
                                                    in0=pb[bk][:, 0:256], scalar=0.0,
                                                    in1=S1[:, base:base + 256],
                                                    op0=ALU.add, op1=ALU.subtract),
                             svv)
                    ln["bankf"][f % 2] = lt_
                ln["pair_tr"] = smm.c

        ln_phase(xp_d, False)

        phase_barrier()
        w_in = A32[:, 0:2048]
        kbuf = A32[:, 2048:4096]
        kvbuf = A32[:, 4096:6144]
        whi = [A16[:, 0:2048], A16[:, 2048:4096]]
        wlo = [A16[:, 4096:6144], A16[:, 6144:8192]]
        vh_hi = A16[:, 8192:8192 + 2048]
        vh_lo = A16[:, 10240:10240 + 2048]

        sp = {"w_free": 0, "whilo_free": [0, 0]}

        def stream_weight_split(wdram, e, par):
            wait("sync", svv, sp["w_free"])
            lt = stWA.dma(w_in[:, 0:D], wdram[e, :, :])
            wait("vector", smm, sp["whilo_free"][par])
            wait_t("vector", lt)
            op("vector", lambda e_, p=par:
               e_.tensor_copy(whi[p][:, 0:D], w_in[:, 0:D]))
            st = op("vector", lambda e_, p=par:
                    e_.scalar_tensor_tensor(out=wlo[p][:, 0:D], in0=w_in[:, 0:D],
                                            scalar=0.0, in1=whi[p][:, 0:D],
                                            op0=ALU.add, op1=ALU.subtract), svv)
            sp["w_free"] = st
            return st

        def proj_mms(par, bank_free, split_t):
            res = []
            wait("tensor", svv, split_t)
            for c in range(NCH):
                bf_val, bf_sem = bank_free[c]
                if bf_val:
                    wait("tensor", CSEM[bf_sem], bf_val)
                for dt in range(NDT):
                    mv = dt * T + c * 512
                    op("tensor", lambda e, c=c, dt=dt, p=par, mv=mv, s=(dt == 0):
                       e.matmul(pb[c][:, :], whi[p][:, dt * 128:(dt + 1) * 128],
                                S1[:, mv:mv + 512], start=s, stop=False))
                    op("tensor", lambda e, c=c, dt=dt, p=par, mv=mv:
                       e.matmul(pb[c][:, :], whi[p][:, dt * 128:(dt + 1) * 128],
                                S2[:, mv:mv + 512], start=False, stop=False))
                    lam = (lambda e, c=c, dt=dt, p=par, mv=mv,
                           s=(dt == NDT - 1):
                           e.matmul(pb[c][:, :],
                                    wlo[p][:, dt * 128:(dt + 1) * 128],
                                    S1[:, mv:mv + 512], start=False, stop=s))
                    if dt == NDT - 1:
                        last_t = topi(lam)
                    else:
                        op("tensor", lam)
                res.append((c, last_t))
            sp["whilo_free"][par] = smm.c
            return res

        bank_free = [(0, "svv")] * NCH
        next_k = next_v = None
        for e in range(NE):
            st_k = next_k if next_k is not None else \
                stream_weight_split(wk_d, e, 0)
            st_v = next_v if next_v is not None else \
                stream_weight_split(wv_d, e, 1)
            kres = proj_mms(0, bank_free, st_k)
            kev = []
            for c, mt in kres:
                wait("scalar", smm, mt)
                kev.append(op("scalar", lambda e_, c=c:
                              e_.copy(out=kbuf[:, c * 512:(c + 1) * 512],
                                      in_=pb[c][:, :]), ssc))
            vres = proj_mms(1, [(kev[c], "ssc") for c in range(NCH)], st_v)
            kvts = []
            for c, mt in vres:
                wait("vector", smm, mt)
                wait("vector", ssc, kev[c])
                kvts.append(op("vector", lambda e_, c=c:
                              e_.tensor_tensor(out=kvbuf[:, c * 512:(c + 1) * 512],
                                               in0=kbuf[:, c * 512:(c + 1) * 512],
                                               in1=pb[c][:, :], op=ALU.mult), svv))
            bank_free = [(kvts[c], "svv") for c in range(NCH)]
            op("vector", lambda e_, e=e:
               e_.tensor_reduce(out=tC[:, e:e + 1], in_=kvbuf[:, 0:T], axis=AX,
                                op=ALU.add), svv)
            if e + 1 < NE:
                next_k = stream_weight_split(wk_d, e + 1, 0)
                next_v = stream_weight_split(wv_d, e + 1, 1)
            mean_step(len(mean_units) // NE + 1)
        mean_step(len(mean_units))

        ph1_mm = smm.c
        phase_barrier()
        ln["pair_tr"] = ph1_mm
        ln["xfree"] = [[svv.c, ssc.c], [svv.c, ssc.c]]
        ln_phase(xm_d, True)

        op("vector", lambda e: e.tensor_scalar(out=tRM[:, 0:NT], in0=tRM[:, 0:NT],
                                               scalar1=EPS, scalar2=None,
                                               op0=ALU.max))
        op("vector", lambda e: e.reciprocal(out=tT2[:, 0:1], in_=consts[:, 2:3]))
        op("vector", lambda e: e.reciprocal(out=tRR[:, 0:NT], in_=tRM[:, 0:NT]))
        op("vector", lambda e: e.tensor_scalar(out=tRR[:, 0:NT], in0=tRR[:, 0:NT],
                                               scalar1=consts[:, 2:3], scalar2=None,
                                               op0=ALU.mult))
        op("vector", lambda e: e.tensor_tensor(out=tRR[:, 0:NT], in0=tRR[:, 0:NT],
                                               in1=tR[:, 0:NT], op=ALU.mult))
        sden = op("scalar", lambda e: e.activation(out=tT2[:, 1:2],
                                                   in_=consts[:, 3:4],
                                                   func=ACT.Sqrt))
        wait("vector", ssc, sden)
        op("vector", lambda e: e.tensor_scalar(out=tS[:, 6:7], in0=tT2[:, 1:2],
                                               scalar1=1.0 / SEQ, scalar2=None,
                                               op0=ALU.mult), svv)

        def split3(src, prow=slice(0, 128)):
            op("vector", lambda e: e.tensor_copy(tSplit[prow, 0:1], src))
            op("vector", lambda e: e.scalar_tensor_tensor(
                out=tT2[prow, 2:3], in0=src, scalar=0.0, in1=tSplit[prow, 0:1],
                op0=ALU.add, op1=ALU.subtract))
            op("vector", lambda e: e.tensor_copy(tSplit[prow, 1:2], tT2[prow, 2:3]))
            op("vector", lambda e: e.scalar_tensor_tensor(
                out=tT2[prow, 3:4], in0=tT2[prow, 2:3], scalar=0.0,
                in1=tSplit[prow, 1:2], op0=ALU.add, op1=ALU.subtract))
            return op("vector", lambda e: e.tensor_copy(tSplit[prow, 2:3],
                                                        tT2[prow, 3:4]), svv)

        for wi in range(3):
            st3 = split3(tACC[:, wi:wi + 1])
            wait("tensor", svv, st3)
            for j in range(2):
                op("tensor", lambda e, j=j: e.matmul(
                    pb[4][0:1, 0:1], ones[:, 0:1], tSplit[:, j:j + 1],
                    start=(j == 0), stop=False))
            mt = topi(lambda e: e.matmul(
                pb[4][0:1, 0:1], ones[:, 0:1], tSplit[:, 2:3],
                start=False, stop=True))
            wait("scalar", smm, mt)
            sc1 = op("scalar", lambda e, wi=wi:
                     e.copy(out=tSum[0:1, wi:wi + 1], in_=pb[4][0:1, 0:1]), ssc)
            wait("vector", ssc, sc1)
            op("vector", lambda e, wi=wi: e.tensor_scalar(
                out=tSum[0:1, wi:wi + 1], in0=tSum[0:1, wi:wi + 1],
                scalar1=NWINV, scalar2=EPS, op0=ALU.mult, op1=ALU.add))
            st3b = split3(tSum[0:1, wi:wi + 1], prow=slice(0, 1))
            wait("tensor", svv, st3b)
            for j in range(2):
                op("tensor", lambda e, j=j: e.matmul(
                    pb[5][:, 0:1], onesr[:, :], tSplit[0:1, j:j + 1],
                    start=(j == 0), stop=False))
            mt2 = topi(lambda e: e.matmul(
                pb[5][:, 0:1], onesr[:, :], tSplit[0:1, 2:3],
                start=False, stop=True))
            wait("scalar", smm, mt2)
            sc2 = op("scalar", lambda e, wi=wi:
                     e.copy(out=tS[:, wi:wi + 1], in_=pb[5][:, 0:1]), ssc)
            wait("vector", ssc, sc2)
            op("vector", lambda e, wi=wi: e.tensor_scalar(
                out=tS[:, 3 + wi:4 + wi], in0=tS[:, wi:wi + 1], scalar1=0.5,
                scalar2=None, op0=ALU.mult))
            op("vector", lambda e, wi=wi: e.tensor_scalar(
                out=tSn[:, wi:wi + 1], in0=tS[:, wi:wi + 1], scalar1=-0.5,
                scalar2=None, op0=ALU.mult))
            op("vector", lambda e, wi=wi: e.tensor_tensor(
                out=tS[:, 8 + wi:9 + wi], in0=tS[:, wi:wi + 1], in1=tT2[:, 0:1],
                op=ALU.mult), svv)
        qu["th_ready"] = svv.c
        op("vector", lambda e: e.tensor_scalar(out=tSCG[:, 0:NT], in0=tRM[:, 0:NT],
                                               scalar1=tS[:, 8:9], scalar2=None,
                                               op0=ALU.mult))
        op("vector", lambda e: e.tensor_scalar(out=tSCU[:, 0:NT], in0=tRM[:, 0:NT],
                                               scalar1=tS[:, 9:10], scalar2=None,
                                               op0=ALU.mult))
        op("vector", lambda e: e.memset(tHM[:, 0:NT], 0.0), svv)

        phase_barrier()
        qbuf = A32[:, 2048:4096]
        kbuf3 = A32[:, 4096:6144]
        kvbuf3 = A32[:, 6144:8192]

        vhat_free = None
        q_consumed = 0
        k_consumed = 0
        bank_free = [(svv.c, "svv")] * NCH
        next_split = None
        next_ksplit = None
        for e in range(NE):
            st_q = next_split if next_split is not None else \
                stream_weight_split(wq_d, e, 0)
            qres = proj_mms(0, bank_free, st_q)
            qev = []
            for c, mt in qres:
                wait("scalar", smm, mt)
                if c == 0:
                    wait("scalar", svv, q_consumed)
                qev.append(op("scalar", lambda e_, c=c:
                              e_.copy(out=qbuf[:, c * 512:(c + 1) * 512],
                                      in_=pb[c][:, :]), ssc))
            st_k = next_ksplit if next_ksplit is not None else \
                stream_weight_split(wk_d, e, 1)
            kres = proj_mms(1, [(qev[c], "ssc") for c in range(NCH)], st_k)
            kev = []
            for c, mt in kres:
                wait("scalar", smm, mt)
                if c == 0:
                    wait("scalar", svv, k_consumed)
                kev.append(op("scalar", lambda e_, c=c:
                              e_.copy(out=kbuf3[:, c * 512:(c + 1) * 512],
                                      in_=pb[c][:, :]), ssc))
            st_v = stream_weight_split(wv_d, e, 0)
            vres = proj_mms(0, [(kev[c], "ssc") for c in range(NCH)], st_v)
            if e + 1 < NE:
                next_split = stream_weight_split(wq_d, e + 1, 0)
                next_ksplit = stream_weight_split(wk_d, e + 1, 1)
            kvts = []
            for c, mt in vres:
                wait("vector", smm, mt)
                wait("vector", ssc, kev[c])
                if c == 0 and vhat_free is not None:
                    wait_t("vector", vhat_free[0])
                    wait_t("vector", vhat_free[1])
                kvts.append(op("vector", lambda e_, c=c:
                              e_.tensor_tensor(out=kvbuf3[:, c * 512:(c + 1) * 512],
                                               in0=kbuf3[:, c * 512:(c + 1) * 512],
                                               in1=pb[c][:, :], op=ALU.mult), svv))
            bank_free = [(kvts[c], "svv") for c in range(NCH)]
            k_consumed = kvts[-1]
            op("vector", lambda e_, e=e: e_.tensor_tensor_scan(
                out=kvbuf3[:, 0:T], data0=kvbuf3[:, 0:T], data1=kvbuf3[:, 0:T],
                initial=tC[:, e:e + 1], op0=ALU.add, op1=ALU.bypass))
            q_consumed = op("vector", lambda e_: e_.scalar_tensor_tensor(
                out=kvbuf3[:, 0:T], in0=qbuf[:, 0:T], scalar=tS[:, 6:7],
                in1=kvbuf3[:, 0:T], op0=ALU.mult, op1=ALU.mult), svv)
            op("vector", lambda e_: e_.tensor_copy(vh_hi[:, 0:T], kvbuf3[:, 0:T]))
            spl = op("vector", lambda e_: e_.scalar_tensor_tensor(
                out=vh_lo[:, 0:T], in0=kvbuf3[:, 0:T], scalar=0.0,
                in1=vh_hi[:, 0:T], op0=ALU.add, op1=ALU.subtract), svv)
            wait("scalar", svv, spl)
            vw1 = stVW.dma(vhat_d[0, e, :, :], vh_hi[:, 0:T])
            vw2 = stVW.dma(vhat_d[1, e, :, :], vh_lo[:, 0:T])
            vhat_free = (vw1, vw2)
            quant_step((2 * len(q_units) // 3 + NE - 1) // NE)

        ph3_mm = smm.c

        phase_barrier()
        wo_in = A32[:, 0:2048]
        stage4 = A32[:, 2048:4096]
        x5sl = [A32[:, 4096:6144]]
        t5 = A32[:, 6144:6144 + QW]
        c5 = A32[:, 7168:7168 + QW]
        d5 = A32[:, 8192:8192 + QW]
        r5 = A32[:, 9216:9216 + QW]
        NSL = D // QW

        vs_hi_sl = [A16[:, 0:NE * 256], A16[:, 2 * NE * 256:3 * NE * 256]]
        vs_lo_sl = [A16[:, NE * 256:2 * NE * 256],
                    A16[:, 3 * NE * 256:4 * NE * 256]]
        NTTG = NT // 2
        NFCW = D // 512

        prep_units = [(tt, s) for tt in range(NT) for s in range(NSL)]
        pstate = {"i": 0, "x_free": 0, "x_load": None, "af_free": 0,
                  "tb_free": [0, 0]}

        def prep_step(n):
            for _ in range(n):
                i = pstate["i"]
                if i >= len(prep_units):
                    return
                tt, s = prep_units[i]
                x5 = x5sl[0]
                if s == 0:
                    wait("sync", svv, pstate["x_free"])
                    pstate["x_load"] = stX.dma(
                        x5[:, 0:D], xm_d[tt * 128:(tt + 1) * 128, 0:D])
                wait_t("vector", pstate["x_load"])
                op("vector", lambda e, s=s, tt=tt, x5=x5: e.tensor_scalar(
                    out=t5, in0=x5[:, s * QW:(s + 1) * QW],
                    scalar1=tM[:, tt:tt + 1], scalar2=tRR[:, tt:tt + 1],
                    op0=ALU.subtract, op1=ALU.mult))
                op("vector", lambda e: e.tensor_copy(AI[:, 0:QW], t5))
                op("vector", lambda e: e.tensor_copy(r5, AI[:, 0:QW]))
                wait("vector", ssc, pstate["af_free"])
                op("vector", lambda e: e.scalar_tensor_tensor(
                    out=c5, in0=t5, scalar=-0.5, in1=r5,
                    op0=ALU.add, op1=ALU.is_gt))
                op("vector", lambda e: e.scalar_tensor_tensor(
                    out=d5, in0=t5, scalar=0.5, in1=r5,
                    op0=ALU.add, op1=ALU.is_lt))
                op("vector", lambda e: e.tensor_tensor(out=t5, in0=r5,
                                                       in1=c5, op=ALU.add))
                op("vector", lambda e: e.tensor_tensor(out=r5, in0=t5, in1=d5,
                                                       op=ALU.subtract))
                cl = op("vector", lambda e: e.tensor_scalar(
                    out=c5, in0=r5, scalar1=127.0, scalar2=-128.0,
                    op0=ALU.min, op1=ALU.max), svv)
                if s == NSL - 1:
                    pstate["x_free"] = cl
                ngr = QW // 512
                for g in range(ngr):
                    tb = (g + s * ngr) % 2
                    wait("tensor", ssc, pstate["tb_free"][tb])
                    wait("tensor", svv, cl)
                    for j in range(4):
                        col = g * 512 + j * 128
                        lam = (lambda e, col=col, tb=tb, j=j:
                               e.transpose(pb[4 + tb][:, j * 128:(j + 1) * 128],
                                           c5[:, col:col + 128], id32[:, :]))
                        if j == 3:
                            mt2 = topi(lam)
                        else:
                            op("tensor", lam)
                    wait("scalar", smm, mt2)
                    dt0 = s * (QW // 128) + g * 4
                    dst = bass.AP(S1, dt0 * T + tt * 128,
                                  [[S1W, 128], [T, 4], [1, 128]])
                    ev = op("scalar", lambda e, tb=tb, dst=dst:
                            e.copy(out=dst, in_=pb[4 + tb][:, 0:512]), ssc)
                    pstate["tb_free"][tb] = ev
                    pstate["af_free"] = ev
                pstate["i"] += 1

        fc_pairs = [tuple(range(p, min(p + 2, NFCW)))
                    for p in range(0, NFCW, 2)]
        bank_free4 = [(0, "ssc")] * 8
        stage_free = [None] * 4
        wo_sweep_done = 0
        n_prep_per_iter = max(1, (len(prep_units) + len(fc_pairs) * NTTG - 1)
                              // (len(fc_pairs) * NTTG))
        for pi, fcl in enumerate(fc_pairs):
            spl_done = 0
            for c2 in range(len(fcl)):
                fc = fcl[c2]
                for hhalf in range(NE * 512 // 2048):
                    wait("sync", svv, spl_done)
                    wait("sync", smm, wo_sweep_done)
                    lt = stWO.dma(wo_in[:, 0:2048],
                                  wo_d[fc, :, hhalf * 2048:(hhalf + 1) * 2048])
                    wait_t("vector", lt)
                    wait("vector", smm, wo_sweep_done)
                    base = c2 * 2 * NE * 512 + hhalf * 2048
                    op("vector", lambda e, base=base:
                       e.tensor_copy(S2[:, base:base + 2048], wo_in[:, 0:2048]))
                    spl_done = op("vector", lambda e, base=base:
                                  e.scalar_tensor_tensor(
                                      out=S2[:, NE * 512 + base:
                                             NE * 512 + base + 2048],
                                      in0=wo_in[:, 0:2048], scalar=0.0,
                                      in1=S2[:, base:base + 2048],
                                      op0=ALU.add, op1=ALU.subtract), svv)
            pair_split_done = spl_done
            for ttg in range(NTTG):
                gi = pi * NTTG + ttg
                par2 = gi % 2
                vs_hi = vs_hi_sl[par2]
                vs_lo = vs_lo_sl[par2]
                if gi == 0:
                    wait("sync", smm, pstate.get("vs_free_%d" % par2, 0))
                    for et in range(NE):
                        stVR.dma(vs_hi[:, et * 256:(et + 1) * 256],
                                 vhat_d[0, et, :, ttg * 256:(ttg + 1) * 256])
                        vs_tok = stVR.dma(
                            vs_lo[:, et * 256:(et + 1) * 256],
                            vhat_d[1, et, :, ttg * 256:(ttg + 1) * 256])
                    pstate["vs_toks_%d" % par2] = stVR.all_toks()
                ngi = gi + 1
                if ngi < len(fc_pairs) * NTTG:
                    npar = ngi % 2
                    nttg = ngi % NTTG
                    wait("sync", smm, pstate.get("vs_free_%d" % npar, 0))
                    for et in range(NE):
                        stVR.dma(vs_hi_sl[npar][:, et * 256:(et + 1) * 256],
                                 vhat_d[0, et, :, nttg * 256:(nttg + 1) * 256])
                        stVR.dma(vs_lo_sl[npar][:, et * 256:(et + 1) * 256],
                                 vhat_d[1, et, :, nttg * 256:(nttg + 1) * 256])
                    pstate["vs_toks_%d" % npar] = stVR.all_toks()
                for tok_ in pstate["vs_toks_%d" % par2]:
                    wait_t("tensor", tok_)
                wait("tensor", svv, pair_split_done)
                for tl_ in range(2):
                    for c2_ in range(len(fcl)):
                        bf_val, bf_sem = bank_free4[tl_ * 2 + c2_]
                        if bf_val:
                            wait("tensor", CSEM[bf_sem], bf_val)
                for et in range(NE):
                    for tl in range(2):
                        sb_h = et * 256 + tl * 128
                        for c2 in range(len(fcl)):
                            bkk = tl * 2 + c2
                            whi_b = c2 * 2 * NE * 512 + et * 512
                            wlo_b = whi_b + NE * 512
                            op("tensor", lambda e, sb_h=sb_h, bkk=bkk, wb=whi_b,
                               vh=vs_hi, s=(et == 0):
                               e.matmul(pb[bkk][:, :], vh[:, sb_h:sb_h + 128],
                                        S2[:, wb:wb + 512], start=s, stop=False))
                            op("tensor", lambda e, sb_h=sb_h, bkk=bkk, wb=wlo_b,
                               vh=vs_hi:
                               e.matmul(pb[bkk][:, :], vh[:, sb_h:sb_h + 128],
                                        S2[:, wb:wb + 512], start=False,
                                        stop=False))
                            lam = (lambda e, sb_h=sb_h, bkk=bkk, wb=whi_b,
                                   vl=vs_lo, s=(et == NE - 1):
                                   e.matmul(pb[bkk][:, :],
                                            vl[:, sb_h:sb_h + 128],
                                            S2[:, wb:wb + 512], start=False,
                                            stop=s))
                            if et == NE - 1 and tl == 1 and c2 == len(fcl) - 1:
                                mt = topi(lam)
                            else:
                                op("tensor", lam)
                for tl in range(2):
                    for c2 in range(len(fcl)):
                        fc = fcl[c2]
                        bkk = tl * 2 + c2
                        si = bkk
                        wait("scalar", smm, mt)
                        wait_t("scalar", stage_free[si])
                        ev = op("scalar", lambda e, bkk=bkk, si=si:
                                e.copy(out=stage4[:, si * 512:(si + 1) * 512],
                                       in_=pb[bkk][:, :]), ssc)
                        bank_free4[bkk] = (ev, "ssc")
                        wait("scalar", ssc, ev)
                        r0 = ttg * 256 + tl * 128
                        stage_free[si] = stOW.dma(
                            out_d[r0:r0 + 128, fc * 512:(fc + 1) * 512],
                            stage4[:, si * 512:(si + 1) * 512])
                pstate["vs_free_%d" % par2] = smm.c
                prep_step(n_prep_per_iter)
            wo_sweep_done = smm.c
        ph4_mm = smm.c
        prep_step(len(prep_units))
        prep_done_sc = ssc.c
        prep_done_vv = svv.c

        phase_barrier()
        gbuf = [A32[:, 0:512], A32[:, 512:1024]]
        hbuf = [A32[:, 1024:1536], A32[:, 1536:2048]]
        SCGb = A32[:, 2048:2048 + T]
        SCUb = A32[:, 2048 + T:2048 + 2 * T]
        HMb = A32[:, 2048 + 2 * T:2048 + 3 * T]
        WSL = NDT * 512
        wg_sl = [S2[:, 0:WSL], S2[:, WSL:2 * WSL]]
        wu_sl = [S2[:, 2 * WSL:3 * WSL], S2[:, 3 * WSL:4 * WSL]]
        assert 4 * WSL <= S2W and 2048 + 3 * T <= 8192

        op("vector", lambda e: e.memset(gbuf[0][:, 0:128], 1.0), svv)
        op("vector", lambda e: e.memset(HMb[:, 0:T], 0.0), svv)
        bc_bank_cp = [0, 0]
        bc_bank_tr = [0, 0]
        for src, dst in ((tSCG, SCGb), (tSCU, SCUb)):
            for tt in range(NT):
                bk = tt % 2
                pcol = 128 + bk * 128
                if bc_bank_tr[bk]:
                    wait("vector", smm, bc_bank_tr[bk])
                bt = op("vector", lambda e, src=src, tt=tt, pcol=pcol:
                        e.tensor_scalar(out=gbuf[0][:, pcol:pcol + 128],
                                        in0=gbuf[0][:, 0:128],
                                        scalar1=src[:, tt:tt + 1],
                                        scalar2=None, op0=ALU.mult), svv)
                wait("tensor", svv, bt)
                wait("tensor", ssc, bc_bank_cp[bk])
                mt = topi(lambda e, bk=bk, pcol=pcol:
                          e.transpose(pb[bk][:, 0:128],
                                      gbuf[0][:, pcol:pcol + 128], id32[:, :]))
                bc_bank_tr[bk] = mt
                wait("scalar", smm, mt)
                bc_bank_cp[bk] = op("scalar", lambda e, bk=bk, dst=dst, tt=tt:
                                    e.copy(out=dst[:, tt * 128:(tt + 1) * 128],
                                           in_=pb[bk][:, 0:128]), ssc)
        bc_done = ssc.c

        def w8_ap(w8d, fc):
            return bass.AP(w8d, fc * NDT * 128 * 512,
                           [[512, 128], [128 * 512, NDT], [1, 512]])

        wgu_free = [0, 0]
        gu_load_t = {}

        def load_gu(fc):
            if fc >= NFC:
                return
            sl = fc % 2
            wait("sync", smm, wgu_free[sl])
            t1 = stQ8R.dma(wg_sl[sl], w8_ap(wg8_d, fc))
            t2 = stQ8R.dma(wu_sl[sl], w8_ap(wu8_d, fc))
            gu_load_t[fc] = (t1, t2)

        load_gu(0)
        load_gu(1)
        pair_free = [0, 0, 0, 0]
        g_free = [0, 0]
        h_free = [None, None]
        first_sig = False
        itg = 0
        for fc in range(NFC):
            sl = fc % 2
            wait_t("tensor", gu_load_t[fc][0])
            wait_t("tensor", gu_load_t[fc][1])
            if fc == 0:
                wait("tensor", ssc, prep_done_sc)
                wait("tensor", ssc, bc_done)
            for fq in range(4):
                ffd = fc * 4 + fq
                for w in range(NCH):
                    pr = itg % 4
                    wait("tensor", svv, pair_free[pr])
                    for dt in range(NDT):
                        lcol = dt * 512 + fq * 128
                        rcol = dt * T + w * 512
                        op("tensor", lambda e, lcol=lcol, rcol=rcol, pr=pr,
                           sl=sl, s=(dt == 0), p=(dt == NDT - 1):
                           e.matmul(pb[pr * 2][:, :],
                                    wg_sl[sl][:, lcol:lcol + 128],
                                    S1[:, rcol:rcol + 512], start=s, stop=p))
                        lam = (lambda e, lcol=lcol, rcol=rcol, pr=pr, sl=sl,
                               s=(dt == 0), p=(dt == NDT - 1):
                               e.matmul(pb[pr * 2 + 1][:, :],
                                        wu_sl[sl][:, lcol:lcol + 128],
                                        S1[:, rcol:rcol + 512],
                                        start=s, stop=p))
                        if dt == NDT - 1:
                            mt = topi(lam)
                        else:
                            op("tensor", lam)
                    gsl = itg % 2
                    hsl = itg % 2
                    wait("vector", smm, mt)
                    wait_t("vector", h_free[hsl])
                    pg = op("vector", lambda e, pr=pr, hsl=hsl, w=w:
                            e.tensor_tensor(out=hbuf[hsl],
                                            in0=pb[pr * 2][:, :],
                                            in1=SCGb[:, w * 512:(w + 1) * 512],
                                            op=ALU.mult), svv)
                    if not first_sig:
                        wait("scalar", svv, prep_done_vv)
                        first_sig = True
                    wait("scalar", svv, pg)
                    wait("scalar", svv, g_free[gsl])
                    sg = op("scalar", lambda e, gsl=gsl, hsl=hsl:
                            e.activation(out=gbuf[gsl], in_=hbuf[hsl],
                                         func=ACT.Sigmoid), ssc)
                    wait("vector", ssc, sg)
                    op("vector", lambda e, pr=pr, hsl=hsl, w=w:
                       e.tensor_tensor(out=hbuf[hsl], in0=pb[pr * 2 + 1][:, :],
                                       in1=SCUb[:, w * 512:(w + 1) * 512],
                                       op=ALU.mult))
                    pu = op("vector", lambda e, hsl=hsl, gsl=gsl:
                            e.tensor_tensor(out=hbuf[hsl], in0=hbuf[hsl],
                                            in1=gbuf[gsl], op=ALU.mult), svv)
                    pair_free[pr] = pu
                    g_free[gsl] = pu
                    op("vector", lambda e, hsl=hsl, w=w:
                       e.tensor_tensor(out=HMb[:, w * 512:(w + 1) * 512],
                                       in0=HMb[:, w * 512:(w + 1) * 512],
                                       in1=hbuf[hsl], op=ALU.max))
                    op("vector", lambda e, hsl=hsl, w=w:
                       e.scalar_tensor_tensor(
                           out=HMb[:, w * 512:(w + 1) * 512],
                           in0=hbuf[hsl], scalar=-1.0,
                           in1=HMb[:, w * 512:(w + 1) * 512],
                           op0=ALU.mult, op1=ALU.max), svv)
                    wait("scalar", svv, pu)
                    h_free[hsl] = stHW.dma(h_d[ffd, :, w * 512:(w + 1) * 512],
                                           hbuf[hsl])
                    itg += 1
                    if itg % 2 == 0:
                        quant_step(1)
            wgu_free[sl] = smm.c
            load_gu(fc + 2)
        quant_step(len(q_units))
        ph5a_mm = smm.c
        hm_last = svv.c

        for pr in range(4):
            wait("tensor", svv, pair_free[pr])
        wait("tensor", svv, hm_last)
        hm_red = [0, 0]
        for tt in range(NT):
            bk = tt % 2
            wait("tensor", svv, hm_red[bk])
            mt = topi(lambda e, bk=bk, tt=tt:
                      e.transpose(pb[bk][:, 0:128],
                                  HMb[:, tt * 128:(tt + 1) * 128], id32[:, :]))
            wait("vector", smm, mt)
            hm_red[bk] = op("vector", lambda e, bk=bk, tt=tt:
                            e.tensor_reduce(out=tHM[:, tt:tt + 1],
                                            in_=pb[bk][:, 0:128], axis=AX,
                                            op=ALU.max), svv)
        hm_tr_done = smm.c


        op("vector", lambda e: e.tensor_scalar(out=tHM[:, 0:NT], in0=tHM[:, 0:NT],
                                               scalar1=EPS, scalar2=None,
                                               op0=ALU.max))
        op("vector", lambda e: e.reciprocal(out=tSCD[:, 0:NT], in_=tHM[:, 0:NT]))
        op("vector", lambda e: e.tensor_scalar(out=tSCD[:, 0:NT],
                                               in0=tSCD[:, 0:NT],
                                               scalar1=consts[:, 2:3],
                                               scalar2=None, op0=ALU.mult))
        op("vector", lambda e: e.tensor_copy(tRR[:, 0:NT], tSCD[:, 0:NT]))
        hth = op("vector", lambda e: e.tensor_scalar(out=tSCD[:, 0:NT],
                                                     in0=tHM[:, 0:NT],
                                                     scalar1=tS[:, 10:11],
                                                     scalar2=None,
                                                     op0=ALU.mult), svv)

        h5sl = [A32[:, 0:1024], A32[:, 1024:2048], A32[:, 2048:3072]]
        t5b = A32[:, 3072:4096]
        c5b = A32[:, 4096:5120]
        r5b = A32[:, 5120:6144]
        d5b = A32[:, 6144:7168]
        xc5 = A32[:, 7168:9216]
        xtemp = A32[:, 7168:7168 + T]
        RRBw = [A32[:, 9216:9472], A32[:, 9472:9728]]
        AHW = NFFD * 256
        buf5 = [S2[:, 0:AHW], S1[:, 0:AHW]]
        wd_sl = [S2[:, AHW:AHW + 4 * D], S2[:, AHW + 4 * D:AHW + 8 * D],
                 S1[:, AHW:AHW + 4 * D]]
        assert AHW + 8 * D <= S2W and AHW + 4 * D <= S1W
        NOC = D // 512
        NST = NT // 2
        NSLAB = NFFD // 4

        wait("vector", ssc, ssc.c)
        op("vector", lambda e: e.memset(gbuf[0][:, 0:128], 1.0), svv)
        rrb_tr = [0, 0]
        rrb_cp = [0, 0]
        for tt in range(NT):
            bk = tt % 2
            pcol = 128 + bk * 128
            if rrb_tr[bk]:
                wait("vector", smm, rrb_tr[bk])
            bt = op("vector", lambda e, tt=tt, pcol=pcol:
                    e.tensor_scalar(out=gbuf[0][:, pcol:pcol + 128],
                                    in0=gbuf[0][:, 0:128],
                                    scalar1=tRR[:, tt:tt + 1],
                                    scalar2=None, op0=ALU.mult), svv)
            wait("tensor", svv, bt)
            wait("tensor", ssc, rrb_cp[bk])
            if tt < 2:
                wait("tensor", svv, hm_red[bk])
            mt = topi(lambda e, bk=bk, pcol=pcol:
                      e.transpose(pb[bk][:, 0:128],
                                  gbuf[0][:, pcol:pcol + 128], id32[:, :]))
            rrb_tr[bk] = mt
            wait("scalar", smm, mt)
            rrb_cp[bk] = op("scalar", lambda e, bk=bk, tt=tt:
                            e.copy(out=xtemp[:, tt * 128:(tt + 1) * 128],
                                   in_=pb[bk][:, 0:128]), ssc)
        rrb_done_sc = ssc.c

        wait("sync", svv, svv.c)
        wait("sync", ssc, ssc.c)
        wait("sync", smm, smm.c)
        wait_all("sync", stQ8W)
        wait("scalar", svv, svv.c)
        wait("scalar", smm, smm.c)
        wait_all("scalar", stHW)
        wait_all("scalar", stQ8W)

        rrb_wr = stX.dma(rrb_d[:, 0:T], xtemp[:, 0:T])
        rrbw_tok = [None, None]
        rrbw_war = [0, 0]

        def load_rrbw(st):
            if st >= NST:
                return
            sl = st % 2
            wait("sync", svv, rrbw_war[sl])
            wait_t("sync", rrb_wr)
            rrbw_tok[sl] = stX.dma(RRBw[sl],
                                   rrb_d[:, st * 256:st * 256 + 256])

        bank5_free = [0] * 8
        wd_free = [ph5a_mm] * 3
        wd_load_t = {}
        h5_free = [0, 0, 0]
        evtok = {}
        bst = {"k": 0, "c5_war": 0}
        xc_free = {"t": 0}
        st_free = [None]
        first_ev = {"done": False}
        wd_items = [(sti, gi) for sti in range(NST) for gi in range(NSLAB)]

        def wd_ap(grp):
            return bass.AP(wd8_d, grp * 4 * 128 * D,
                           [[D, 128], [128 * D, 4], [1, D]])

        def load_wd_n(n):
            if n >= len(wd_items):
                return
            sl = n % 3
            wait("sync", smm, wd_free[sl])
            wd_load_t[n] = stQ8R.dma(wd_sl[sl], wd_ap(wd_items[n][1]))

        def build_slab(st, s):
            if st >= NST:
                return
            hs = bst["k"] % 3
            bst["k"] += 1
            hslot = h5sl[hs]
            wait("scalar", svv, h5_free[hs])
            rb = stHR.dma(hslot, bass.AP(h_d, (4 * s) * 128 * T + st * 256,
                                         [[T, 128], [128 * T, 4], [1, 256]]))
            wait_t("vector", rb)
            wait_t("vector", rrbw_tok[st % 2])
            t1 = 0
            for q in range(4):
                t1 = op("vector", lambda e, q=q, hslot=hslot, st=st:
                        e.tensor_tensor(out=t5b[:, q * 256:(q + 1) * 256],
                                        in0=hslot[:, q * 256:(q + 1) * 256],
                                        in1=RRBw[st % 2][:, 0:256],
                                        op=ALU.mult), svv)
            h5_free[hs] = t1
            rrbw_war[st % 2] = t1
            op("vector", lambda e: e.tensor_copy(AI[:, 0:1024], t5b))
            op("vector", lambda e: e.tensor_copy(r5b, AI[:, 0:1024]))
            wait("vector", ssc, bst["c5_war"])
            op("vector", lambda e: e.scalar_tensor_tensor(
                out=c5b, in0=t5b, scalar=-0.5, in1=r5b,
                op0=ALU.add, op1=ALU.is_gt))
            op("vector", lambda e: e.tensor_tensor(out=c5b, in0=r5b, in1=c5b,
                                                   op=ALU.add))
            op("vector", lambda e: e.scalar_tensor_tensor(
                out=d5b, in0=t5b, scalar=0.5, in1=r5b,
                op0=ALU.add, op1=ALU.is_lt))
            t7 = op("vector", lambda e: e.tensor_tensor(out=c5b, in0=c5b,
                                                        in1=d5b,
                                                        op=ALU.subtract), svv)
            wait("scalar", svv, t7)
            t9 = op("scalar", lambda e, st=st, s=s:
                    e.copy(out=buf5[st % 2][:, s * 1024:(s + 1) * 1024],
                           in_=c5b), ssc)
            bst["c5_war"] = t9
            evtok[(st, s)] = t9

        def sweep_grp(st, g):
            n = st * NSLAB + g
            sl = n % 3
            wait("tensor", ssc, evtok[(st, g)])
            wait_t("tensor", wd_load_t[n])
            if g == 0:
                if st == 0:
                    wait("tensor", ssc, rrb_done_sc)
                for bkk in range(2 * NOC):
                    wait("tensor", svv, bank5_free[bkk])
            for sub in range(4):
                ffd = g * 4 + sub
                for tl in range(2):
                    lcol = ffd * 256 + tl * 128
                    for oc in range(NOC):
                        lam = (lambda e, lcol=lcol, tl=tl, oc=oc, sl=sl, st=st,
                               sub=sub, s=(ffd == 0), p=(ffd == NFFD - 1):
                               e.matmul(pb[tl * NOC + oc][:, :],
                                        buf5[st % 2][:, lcol:lcol + 128],
                                        wd_sl[sl][:, sub * D + oc * 512:
                                                  sub * D + oc * 512 + 512],
                                        start=s, stop=p))
                        if sub == 3 and tl == 1 and oc == NOC - 1:
                            wd_free[sl] = topi(lam)
                        else:
                            op("tensor", lam)

        def evac(st, sweep_mt):
            for tl in range(2):
                tt = st * 2 + tl
                wait("sync", svv, xc_free["t"])
                if not first_ev["done"]:
                    wait("sync", smm, hm_tr_done)
                    wait_t("sync", rrb_wr)
                    first_ev["done"] = True
                wait_t("sync", st_free[0])
                xt = stX.dma(xc5[:, 0:D], xm_d[tt * 128:(tt + 1) * 128, 0:D])
                wait("vector", smm, sweep_mt)
                wait_t("vector", xt)
                wait("vector", svv, hth)
                evt = 0
                for oc in range(NOC):
                    bkk = tl * NOC + oc
                    evt = op("vector", lambda e, bkk=bkk, oc=oc, tt=tt:
                             e.scalar_tensor_tensor(
                                 out=xc5[:, oc * 512:(oc + 1) * 512],
                                 in0=pb[bkk][:, :],
                                 scalar=tSCD[:, tt:tt + 1],
                                 in1=xc5[:, oc * 512:(oc + 1) * 512],
                                 op0=ALU.mult, op1=ALU.add), svv)
                    bank5_free[bkk] = evt
                xc_free["t"] = evt
                wait("gpsimd", svv, evt)
                st_free[0] = stOA.dma(out_d[tt * 128:(tt + 1) * 128, 0:D],
                                      xc5[:, 0:D], accum_op=ALU.add)

        SERIAL5B = False
        if SERIAL5B:
            for n0 in range(3):
                load_wd_n(n0)
            for st in range(NST):
                load_rrbw(st)
                for s in range(NSLAB):
                    build_slab(st, s)
                for k in range(NSLAB):
                    sweep_grp(st, k)
                    load_wd_n(st * NSLAB + k + 3)
                evac(st, smm.c)
        else:
            for n0 in range(3):
                load_wd_n(n0)
            load_rrbw(0)
            wait("vector", smm, hm_tr_done)
            for s in range(NSLAB):
                build_slab(0, s)
            EV_AT = max(1, NSLAB * 3 // 8)
            for st in range(NST):
                load_rrbw(st + 1)
                for k in range(NSLAB):
                    sweep_grp(st, k)
                    load_wd_n(st * NSLAB + k + 3)
                swmt = smm.c
                for k in range(NSLAB):
                    build_slab(st + 1, k)
                    if k == EV_AT:
                        evac(st, swmt)

        wait_all("gpsimd", stOA)
        wait_all("gpsimd", stOW)
        wait_all("sync", stOA)

        with nc.Block() as block:
            @block.sync
            def _(e):
                for f in S["sync"]:
                    f(e)

            @block.tensor
            def _(e):
                for f in S["tensor"]:
                    f(e)

            @block.vector
            def _(e):
                for f in S["vector"]:
                    f(e)

            @block.scalar
            def _(e):
                for f in S["scalar"]:
                    f(e)

            @block.gpsimd
            def _(e):
                for f in S["gpsimd"]:
                    f(e)

    return nc


_PROG = {}


def _get_prog(cfg_key):
    if cfg_key not in _PROG:
        _PROG[cfg_key] = build_program(dict(zip(("D", "T", "FF"), cfg_key)))
    return _PROG[cfg_key]


def make_host_inputs(cfg, Wq, Wk, Wv, Wo, Wg, Wu, Wdn):
    D, T, FF = cfg["D"], cfg["T"], cfg["FF"]
    NE = D // 128
    NDT = D // 128
    NFFD = FF // 128

    def att_img(W):
        a = np.ascontiguousarray(W.T).reshape(NDT, 128, NE, 128)
        return np.ascontiguousarray(a.transpose(2, 1, 0, 3).reshape(NE, 128, D))

    def mov_img(W, nblk):
        wt = np.ascontiguousarray(W.T)
        return np.ascontiguousarray(wt.reshape(nblk, 128, wt.shape[1]))

    d = {}
    d["wq"] = att_img(Wq)
    d["wk"] = att_img(Wk)
    d["wv"] = att_img(Wv)
    NFCW = D // 512
    wot = np.ascontiguousarray(Wo.T).reshape(NE, 128, NFCW, 512)
    d["wo"] = np.ascontiguousarray(
        wot.transpose(2, 1, 0, 3).reshape(NFCW, 128, NE * 512))
    d["wg"] = mov_img(Wg, NDT)
    d["wu"] = mov_img(Wu, NDT)
    d["wd"] = mov_img(Wdn, NFFD)
    d["id32"] = np.eye(128, dtype=np.float32)
    d["ones16"] = np.ones((128, 2), dtype=ml_dtypes.bfloat16)
    d["onesr16"] = np.ones((1, 128), dtype=ml_dtypes.bfloat16)
    consts = np.zeros((128, 8), dtype=np.float32)
    consts[:, 0] = EPS
    consts[:, 1] = 1.0 / D
    consts[:, 2] = 127.0
    consts[:, 3] = float(D)
    consts[:, 4] = 1.0
    d["consts"] = consts
    return d


def kernel(L0_attn_x, L0_attn_Wq, L0_attn_Wk, L0_attn_Wv, L0_attn_Wo,
           L0_ffn_w_gate, L0_ffn_w_up, L0_ffn_w_down):
    cfg = full_cfg()
    D, T, FF = cfg["D"], cfg["T"], cfg["FF"]
    x = np.ascontiguousarray(np.asarray(L0_attn_x, dtype=np.float32))
    B_, S_, D_ = x.shape
    assert D_ == D and S_ == 2 * T

    shared = make_host_inputs(
        cfg,
        np.asarray(L0_attn_Wq, np.float32), np.asarray(L0_attn_Wk, np.float32),
        np.asarray(L0_attn_Wv, np.float32), np.asarray(L0_attn_Wo, np.float32),
        np.asarray(L0_ffn_w_gate, np.float32),
        np.asarray(L0_ffn_w_up, np.float32),
        np.asarray(L0_ffn_w_down, np.float32))

    zeros_pref = np.zeros((T, D), dtype=np.float32)
    in_maps = []
    for c in range(8):
        b, h = c // 2, c % 2
        m = dict(shared)
        m["xm"] = np.ascontiguousarray(x[b, h * T:(h + 1) * T])
        m["xp"] = np.ascontiguousarray(x[b, 0:T]) if h == 1 else zeros_pref
        in_maps.append(m)

    nc_prog = _get_prog((D, T, FF))
    res = run_bass_kernel_spmd(nc_prog, in_maps, core_ids=list(range(8)))
    out = np.empty((B_, S_, D), dtype=np.float32)
    for c in range(8):
        b, h = c // 2, c % 2
        out[b, h * T:(h + 1) * T] = np.asarray(res.results[c]["out"])
    return out
